# version 10
# speedup vs baseline: 1.1903x; 1.0044x over previous
import math
import os
from contextlib import ExitStack

import numpy as np
import ml_dtypes
import concourse.bass as bass
import concourse.mybir as mybir
from concourse.bass_utils import run_bass_kernel_spmd

F32 = mybir.dt.float32
BF16 = mybir.dt.bfloat16
AF = mybir.ActivationFunctionType
ALU = mybir.AluOpType
AX = mybir.AxisListType

D = 1024
T = 4096
NC = 8
TT = 512
NT = T // TT
P = 128
TBB = 512

BF = ml_dtypes.bfloat16

ENGS = ("sync", "tensor", "vector", "scalar", "gpsimd")
LAST_REC = None
DBG_NO_MASK = False
DBG_RAW_SCORES = False


class Sem:
    def __init__(self, nc, es, name):
        self.h = es.enter_context(nc.semaphore(name))
        self.n = 0


class Rec:

    def __init__(self, nc, es):
        self.nc = nc
        self.es = es
        self.ops = {e: [] for e in ENGS}
        self.sems = {}

    def _semname(self, s):
        for k, v in self.sems.items():
            if v is s:
                return k
        return "?"

    def sem(self, name):
        if name not in self.sems:
            self.sems[name] = Sem(self.nc, self.es, name)
        return self.sems[name]

    AUTO = {"vector": "ev", "scalar": "esc"}

    def op(self, eng, make, inc=None, inc_n=1, wait=()):
        if not hasattr(self, "log"):
            self.log = []
        self.log.append((eng, len(self.ops[eng]),
                         [(nm, v) for (s, v), nm in
                          zip(wait, [self._semname(s) for s, _ in wait])],
                         self._semname(inc) if inc else None, inc.n + inc_n if inc else None))
        waits = [(s.h, v) for s, v in wait]
        if eng in self.AUTO:
            auto = self.sem(self.AUTO[eng])
            assert inc is None or inc is auto, "auto-serialized engine: only its own sem"
            prev = auto.n
            auto.n += 1
            incs = (auto.h, 1)
            waits = [(auto.h, prev)] + waits
        elif inc is not None:
            inc.n += inc_n
            incs = (inc.h, inc_n)
        else:
            incs = None

        def fn(e):
            for h, v in waits:
                e.wait_ge(h, v)
            ins = make(e)
            if incs is not None:
                ins.then_inc(*incs)
        self.ops[eng].append(fn)

    def wait(self, eng, wait):
        waits = [(s.h, v) for s, v in wait]

        def fn(e):
            for h, v in waits:
                e.wait_ge(h, v)
        self.ops[eng].append(fn)

    def barrier(self):
        pb = self.sem("pbar")
        pb.n += 4 + 16
        target = pb.n
        h = pb.h
        bt = self._bar_tile
        bw = self._bar_w

        def f_sync(eng):
            eng.dma_start(out=bt[0:1, 3:4], in_=bt[0:1, 2:3]).then_inc(h, 16)
            eng.wait_ge(h, target)

        def f_tensor(eng):
            eng.ldweights(bw[0:1, 0:1]).then_inc(h, 1)
            eng.wait_ge(h, target)

        def f_vector(eng):
            eng.engine_nop().then_inc(h, 1)
            eng.wait_ge(h, target)

        def f_scalar(eng):
            eng.copy(bt[0:1, 1:2], bt[0:1, 0:1]).then_inc(h, 1)
            eng.wait_ge(h, target)

        def f_gpsimd(eng):
            eng.engine_nop().then_inc(h, 1)
            eng.wait_ge(h, target)

        self.ops["sync"].append(f_sync)
        self.ops["tensor"].append(f_tensor)
        self.ops["vector"].append(f_vector)
        self.ops["scalar"].append(f_scalar)
        self.ops["gpsimd"].append(f_gpsimd)

    def replay(self):
        block = self.es.enter_context(self.nc.Block())
        ops = self.ops

        @block.sync
        def _(e):
            for f in ops["sync"]:
                f(e)

        @block.tensor
        def _(e):
            for f in ops["tensor"]:
                f(e)

        @block.vector
        def _(e):
            for f in ops["vector"]:
                f(e)

        @block.scalar
        def _(e):
            for f in ops["scalar"]:
                f(e)

        @block.gpsimd
        def _(e):
            for f in ops["gpsimd"]:
                f(e)


def emit_ffn(r: Rec, es, nc, x_in, x_out, y_scr, wdr, C, tag, dbg=None):
    LD = [r.sem("ld0"), r.sem("ld1")]
    S_LW = r.sem("lw")
    S_T = r.sem("et")
    S_V = r.sem("ev")
    S_SC = r.sem("esc")
    ST = [r.sem("st0"), r.sem("st1")]
    YL = [r.sem("yl0"), r.sem("yl1")]

    NMID = 15
    NROUND = 12
    KSEL = 3072

    sb = lambda name, shape, dt: es.enter_context(nc.sbuf_tensor(name + tag, shape, dt))
    XT = [sb(f"xt{i}", [P, NC * TT], F32) for i in range(2)]
    XS = sb("xs", [P, NC * 1024], BF16)
    XLO = sb("xlo", [P, NC * TT], BF16)
    ZB = sb("zb", [P, NC * T], BF16)
    WB = sb("wb", [P, NC * 1024], BF16)
    WST = sb("wst", [P, 1024], F32)
    WR = sb("wr", [P, 2 * NC], BF16)
    WRF = sb("wrf", [P, NC], F32)
    YT = [sb(f"yt{i}", [P, NC * TT], BF16) for i in range(2)]
    ROWS = sb("rows", [1, 8 * TT], F32)
    RB = sb("rb", [1, TT], BF16)
    SCR = sb("scores", [1, T], F32)
    SPP = sb("spp", [P, 32], F32)
    EVR = sb("evr", [P, TT], BF16)
    EVB = sb("evb", [P, TT], BF16)
    MSKB = sb("mskb", [P, TT], mybir.dt.uint8)
    SEL = sb("sel", [P, NC * TT], BF16)
    BIS = sb("bis", [P, NMID * 32], F32)
    BCNT = sb("bcnt", [P, 16], F32)
    MIDS = sb("mids", [P, 16], F32)
    BR = sb("br", [1, 64], F32)

    psd = lambda name, shape: es.enter_context(nc.psum_tensor(name + tag, shape, F32))
    PS_SX = psd("ps_sx", [1, TT])
    PS_SS = psd("ps_ss", [1, TT])
    PS_R = psd("ps_r", [1, TT])
    PS_BC = psd("ps_bc", [P, TT])
    PS_BC2 = psd("ps_bc2", [P, TT])
    PS_Y = [psd(f"ps_y{i}", [P, TT]) for i in range(2)]
    PS_TR = psd("ps_tr", [P, 4])

    row = lambda off, n=TT: ROWS[0:1, off:off + n]
    R_M, R_MS, R_VAR, R_R, R_BN, R_T1, R_T2, R_MISC = (i * TT for i in range(8))
    mrow = lambda i, n=1: ROWS[0:1, R_MISC + i:R_MISC + i + n]

    x3 = lambda ap: ap.rearrange("(c p) t -> p c t", p=P)
    XS4 = XS[:].rearrange("p (c two q) -> p c two q", two=2, q=TT)
    XLO3 = XLO[:].rearrange("p (c q) -> p c q", c=NC)
    ZB3 = ZB[:].rearrange("p (c t) -> p c t", c=NC)

    r.op("vector", lambda e: e.memset(ROWS[:], 0.0))
    r.op("vector", lambda e: e.memset(BR[:], 0.0))

    for c in range(NC):
        wl = [(S_V, S_V.n)] if c > 0 else []
        r.op("sync", lambda e, c=c: e.dma_start(
            out=WST[:], in_=wdr["w_block"][c * P:(c + 1) * P, :]),
            inc=S_LW, inc_n=16, wait=wl)
        r.op("vector", lambda e, c=c: e.tensor_copy(
            WB[:, c * 1024:(c + 1) * 1024], WST[:]), inc=S_V,
            wait=[(S_LW, S_LW.n)])
    r.op("sync", lambda e: e.dma_start(out=WRF[:], in_=wdr["w_router"]),
         inc=S_LW, inc_n=16, wait=[(S_V, S_V.n)])
    r.op("vector", lambda e: e.tensor_copy(WR[:, 0:NC], WRF[:]),
         wait=[(S_LW, S_LW.n)], inc=S_V)
    r.op("vector", lambda e: e.tensor_tensor(
        WR[:, NC:2 * NC], WRF[:], WR[:, 0:NC], ALU.subtract), inc=S_V)
    r.op("vector", lambda e: e.reduce_sum(BIS[:, 0:1], WRF[:], AX.X), inc=S_V)
    r.op("tensor", lambda e: e.matmul(PS_R[0:1, 0:1], C["ones128_f"], BIS[:, 0:1],
                                      start=True, stop=True), inc=S_T,
         wait=[(S_V, S_V.n)])
    r.op("vector", lambda e: e.tensor_copy(mrow(0), PS_R[0:1, 0:1]),
         inc=S_V, wait=[(S_T, S_T.n)])

    va = {}
    sc_done = 0
    for t in range(NT):
        buf = XT[t % 2]
        S_LD = LD[t % 2]
        buf3 = buf[:].rearrange("p (c q) -> p c q", c=NC)
        wl = [(S_V, va[t - 2])] if (t - 2) in va else []
        r.op("sync", lambda e, t=t, buf=buf: e.dma_start(
            out=buf[:].rearrange("p (c q) -> p c q", c=NC),
            in_=x3(x_in)[:, :, t * TT:(t + 1) * TT]), inc=S_LD, inc_n=16, wait=wl)
        ld_t = S_LD.n

        wcast = [(S_LD, ld_t)]
        if t > 0:
            wcast.append((S_SC, sc_done))
        r.op("vector", lambda e, buf3=buf3: e.tensor_copy(XS4[:, :, 0, :], buf3),
             wait=wcast)
        r.op("vector", lambda e, buf3=buf3: e.tensor_tensor(
            XLO3, buf3, XS4[:, :, 0, :], ALU.subtract), inc=S_V)
        xcast_v = S_V.n

        r.op("scalar", lambda e: e.activation(XS4[:, :, 1, :], XS4[:, :, 0, :],
                                              AF.Square), inc=S_SC,
             wait=[(S_V, xcast_v)])
        sq_s = S_SC.n

        for c in range(NC):
            r.op("tensor", lambda e, c=c: e.matmul(
                PS_SX[:], C["ones128_b"], XS4[:, c, 0, :],
                start=(c == 0), stop=(c == NC - 1)),
                wait=[(S_V, xcast_v)] if c == 0 else [])
        for c in range(NC):
            r.op("tensor", lambda e, c=c: e.matmul(
                PS_SS[:], C["ones128_b"], XS4[:, c, 1, :],
                start=(c == 0), stop=(c == NC - 1)),
                wait=[(S_SC, sq_s)] if c == 0 else [])
        nmm = 0
        for c in range(NC):
            for (wi, xi) in ((0, 0), (1, 0), (0, 1)):
                nmm += 1
                r.op("tensor", lambda e, c=c, wi=wi, xi=xi, nmm=nmm: e.matmul(
                    PS_R[:], WR[:, wi * NC + c:wi * NC + c + 1],
                    XS4[:, c, 0, :] if xi == 0 else XLO3[:, c, :],
                    start=(nmm == 1), stop=(nmm == 24)),
                    inc=(S_T if nmm == 24 else None))
        stats_t = S_T.n

        r.op("vector", lambda e: e.tensor_scalar(row(R_M), PS_SX[:], 1.0 / D, None,
                                                 ALU.mult), wait=[(S_T, stats_t)])
        r.op("vector", lambda e: e.tensor_scalar(row(R_MS), PS_SS[:], 1.0 / D, None,
                                                 ALU.mult))
        r.op("vector", lambda e: e.tensor_tensor(row(R_T1), row(R_M), row(R_M),
                                                 ALU.mult))
        r.op("vector", lambda e: e.tensor_tensor(row(R_VAR), row(R_MS), row(R_T1),
                                                 ALU.subtract))
        r.op("vector", lambda e: e.tensor_scalar(row(R_VAR), row(R_VAR), 1e-5, None,
                                                 ALU.add), inc=S_V)
        var_v = S_V.n
        r.op("scalar", lambda e: e.activation(row(R_T2), row(R_VAR), AF.Sqrt),
             inc=S_SC, wait=[(S_V, var_v)])
        r.op("vector", lambda e: e.reciprocal(row(R_R), row(R_T2)),
             wait=[(S_SC, S_SC.n)])
        r.op("vector", lambda e: e.tensor_tensor(row(R_T1), row(R_M), row(R_R),
                                                 ALU.mult))
        r.op("vector", lambda e: e.tensor_scalar(row(R_BN), row(R_T1), -1.0, None,
                                                 ALU.mult))
        r.op("vector", lambda e: e.tensor_scalar(row(R_T2), row(R_M), mrow(0), None,
                                                 ALU.mult))
        r.op("vector", lambda e: e.tensor_tensor(row(R_T1), PS_R[:], row(R_T2),
                                                 ALU.subtract))
        if DBG_RAW_SCORES:
            r.op("vector", lambda e, t=t: e.tensor_copy(
                SCR[0:1, t * TT:(t + 1) * TT], PS_R[:]))
        else:
            r.op("vector", lambda e, t=t: e.tensor_tensor(
                SCR[0:1, t * TT:(t + 1) * TT], row(R_T1), row(R_R), ALU.mult))
        r.op("vector", lambda e: e.tensor_copy(RB[:], row(R_R)), inc=S_V)
        rb_v = S_V.n
        r.op("tensor", lambda e: e.matmul(PS_BC[:], C["ones1_b"], RB[:],
                                          start=True, stop=True), inc=S_T,
             wait=[(S_V, rb_v)])
        r.op("vector", lambda e: e.tensor_copy(EVR[:], PS_BC[:]),
             wait=[(S_T, S_T.n)], inc=S_V)
        r.op("vector", lambda e: e.tensor_copy(RB[:], row(R_BN)), inc=S_V)
        rb2_v = S_V.n
        r.op("tensor", lambda e: e.matmul(PS_BC2[:], C["ones1_b"], RB[:],
                                          start=True, stop=True), inc=S_T,
             wait=[(S_V, rb2_v)])
        r.op("vector", lambda e: e.tensor_copy(EVB[:], PS_BC2[:]),
             wait=[(S_T, S_T.n)])
        for c in range(NC):
            r.op("vector", lambda e, c=c: e.tensor_tensor(
                SEL[:, c * TT:(c + 1) * TT], XS4[:, c, 0, :], EVR[:], ALU.mult))
        for c in range(NC):
            r.op("vector", lambda e, c=c, t=t: e.tensor_tensor(
                ZB3[:, c, t * TT:(t + 1) * TT], SEL[:, c * TT:(c + 1) * TT],
                EVB[:], ALU.add), inc=(S_V if c == NC - 1 else None))
        va[t] = S_V.n
        z_v = S_V.n

        yb = YT[t % 2]
        for j in range(NC):
            pb = PS_Y[j % 2]
            for c in range(NC):
                w0 = []
                if j == 0 and c == 0:
                    w0.append((S_V, z_v))
                if c == 0 and (j >= 2 or t > 0):
                    w0.append((S_SC, S_SC.n - 1 if j >= 2 else S_SC.n))
                r.op("tensor", lambda e, j=j, c=c, t=t, pb=pb: e.matmul(
                    pb[:], WB[:, c * 1024 + j * P:c * 1024 + (j + 1) * P],
                    ZB3[:, c, t * TT:(t + 1) * TT],
                    start=(c == 0), stop=(c == NC - 1)),
                    inc=(S_T if c == NC - 1 else None), wait=w0)
            ymm_t = S_T.n
            wst = [(ST[t % 2], ST[t % 2].n)] if (t >= 2 and j == 0) else []
            r.op("scalar", lambda e, j=j, pb=pb, yb=yb: e.activation(
                yb[:, j * TT:(j + 1) * TT], pb[:], AF.Gelu_apprx_tanh),
                inc=S_SC, wait=[(S_T, ymm_t)] + wst)
        sc_done = S_SC.n
        r.op("gpsimd", lambda e, t=t, yb=yb: e.dma_start(
            out=x3(y_scr)[:, :, t * TT:(t + 1) * TT],
            in_=yb[:].rearrange("p (c q) -> p c q", c=NC)),
            inc=ST[t % 2], inc_n=16, wait=[(S_SC, sc_done)])

    for i in range(32):
        wl = [(S_V, S_V.n)]
        r.op("tensor", lambda e, i=i: e.transpose(
            PS_TR[:, i % 4:i % 4 + 1], SCR[0:1, i * P:(i + 1) * P], C["ones1_f1"]),
            inc=S_T, wait=wl)
        r.op("vector", lambda e, i=i: e.tensor_copy(SPP[:, i:i + 1],
                                                    PS_TR[:, i % 4:i % 4 + 1]),
             inc=S_V, wait=[(S_T, S_T.n)])

    r.op("vector", lambda e: e.memset(mrow(1), -1.0e6))
    r.op("vector", lambda e: e.memset(mrow(2), 1.0e6), inc=S_V)

    for it in range(NROUND):
        r.op("vector", lambda e: e.tensor_tensor(mrow(3), mrow(2), mrow(1),
                                                 ALU.subtract))
        r.op("vector", lambda e: e.tensor_scalar(mrow(3), mrow(3),
                                                 1.0 / (NMID + 1), None, ALU.mult),
             inc=S_V)
        st_v = S_V.n
        r.op("vector", lambda e: e.tensor_scalar(BR[0:1, 0:NMID], C["iota15"],
                                                 mrow(3), mrow(1), ALU.mult, ALU.add),
             wait=[(S_V, st_v)], inc=S_V)
        mids_v = S_V.n
        if it == 0:
            r.op("vector", lambda e: e.tensor_copy(mrow(128, 4), mrow(1, 4)))
        r.op("tensor", lambda e: e.matmul(PS_BC[:, 0:NMID], C["ones1_f"],
                                          BR[0:1, 0:NMID], start=True, stop=True),
             inc=S_T, wait=[(S_V, mids_v)])
        bcm_t = S_T.n
        r.op("vector", lambda e: e.tensor_copy(MIDS[:, 0:NMID], PS_BC[:, 0:NMID]),
             inc=S_V, wait=[(S_T, bcm_t)])
        mev_v = S_V.n
        for i in range(NMID):
            r.op("vector", lambda e, i=i: e.tensor_scalar(
                BIS[:, i * 32:(i + 1) * 32], SPP[:], MIDS[:, i:i + 1], None,
                ALU.is_ge), wait=[(S_V, mev_v)] if i == 0 else [])
        r.op("vector", lambda e: e.tensor_reduce(
            BCNT[:, 0:NMID], BIS[:].rearrange("p (i q) -> p i q", i=NMID),
            AX.X, ALU.add), inc=S_V)
        red_v = S_V.n
        r.op("tensor", lambda e: e.matmul(PS_SX[0:1, 0:NMID], C["ones128_f"],
                                          BCNT[:, 0:NMID], start=True, stop=True),
             inc=S_T, wait=[(S_V, red_v)])
        cnt_t = S_T.n
        r.op("vector", lambda e: e.tensor_scalar(BR[0:1, 32:32 + NMID],
                                                 PS_SX[0:1, 0:NMID], float(KSEL),
                                                 None, ALU.is_ge),
             wait=[(S_T, cnt_t)])
        r.op("vector", lambda e: e.reduce_sum(mrow(4), BR[0:1, 32:32 + NMID], AX.X))
        r.op("vector", lambda e: e.tensor_tensor(mrow(4), mrow(4), mrow(3), ALU.mult))
        r.op("vector", lambda e: e.tensor_tensor(mrow(1), mrow(1), mrow(4), ALU.add))
        r.op("vector", lambda e: e.tensor_tensor(mrow(2), mrow(1), mrow(3), ALU.add),
             inc=S_V)
        r.op("vector", lambda e, it=it: e.tensor_copy(mrow(100 + 4 * it, 4), mrow(1, 4)))
        if it == 0:
            r.op("vector", lambda e: e.tensor_copy(mrow(40, 15), PS_SX[0:1, 0:NMID]))
            r.op("vector", lambda e: e.tensor_copy(mrow(60, 15), BR[0:1, 0:NMID]))
            r.op("vector", lambda e: e.tensor_copy(mrow(80, 15), BR[0:1, 32:32 + NMID]))
    r.op("vector", lambda e: e.tensor_copy(mrow(5), mrow(1)), inc=S_V)
    tau_v = S_V.n
    if not DBG_NO_MASK:
        r.op("vector", lambda e: e.tensor_scalar(SCR[:], SCR[:], mrow(5), None,
                                                 ALU.is_ge), inc=S_V,
             wait=[(S_V, tau_v)])
    mask_v = S_V.n

    st_totals = [ST[0].n, ST[1].n]
    for t in range(NT):
        buf = XT[t % 2]
        yb = YT[t % 2]
        S_LD = LD[t % 2]
        S_ST = ST[t % 2]
        S_YL = YL[t % 2]
        xwl = [(S_V, S_V.n)] if t < 2 else [(S_ST, S_ST.n)]
        r.op("sync", lambda e, t=t, buf=buf: e.dma_start(
            out=buf[:].rearrange("p (c q) -> p c q", c=NC),
            in_=x3(x_in)[:, :, t * TT:(t + 1) * TT]), inc=S_LD, inc_n=16,
            wait=xwl)
        ldb_t = S_LD.n
        ywl = [(S_ST, st_totals[t % 2])] if t < 2 else [(S_V, S_V.n)]
        r.op("sync", lambda e, t=t, yb=yb: e.dma_start(
            out=yb[:].rearrange("p (c q) -> p c q", c=NC),
            in_=x3(y_scr)[:, :, t * TT:(t + 1) * TT]), inc=S_YL, inc_n=16,
            wait=ywl)
        yl_t = S_YL.n
        r.op("tensor", lambda e, t=t: e.matmul(
            PS_BC[:], C["ones1_f"], SCR[0:1, t * TT:(t + 1) * TT],
            start=True, stop=True), inc=S_T, wait=[(S_V, S_V.n)])
        mb_t = S_T.n
        r.op("vector", lambda e: e.tensor_copy(MSKB[:], PS_BC[:]),
             wait=[(S_T, mb_t)])
        for c in range(NC):
            r.op("vector", lambda e, c=c, t=t: e.tensor_copy(
                SEL[:, c * TT:(c + 1) * TT], ZB3[:, c, t * TT:(t + 1) * TT]))
        for c in range(NC):
            r.op("vector", lambda e, c=c, yb=yb: e.copy_predicated(
                SEL[:, c * TT:(c + 1) * TT], MSKB[:],
                yb[:, c * TT:(c + 1) * TT]),
                wait=[(S_YL, yl_t)] if c == 0 else [])
        r.op("vector", lambda e, buf=buf: e.tensor_tensor(
            buf[:], buf[:], SEL[:], ALU.add), inc=S_V,
            wait=[(S_LD, ldb_t)])
        out_v = S_V.n
        r.op("gpsimd", lambda e, t=t, buf=buf: e.dma_start(
            out=x3(x_out)[:, :, t * TT:(t + 1) * TT],
            in_=buf[:].rearrange("p (c q) -> p c q", c=NC)),
            inc=S_ST, inc_n=16, wait=[(S_V, out_v)])

    if dbg is not None:
        S_DB = ST[0]
        r.op("gpsimd", lambda e: e.dma_start(out=dbg["z"], in_=ZB[:]),
             inc=S_DB, inc_n=16, wait=[(S_V, S_V.n)])
        r.op("gpsimd", lambda e: e.dma_start(out=dbg["scr"], in_=SCR[:]),
             inc=S_DB, inc_n=16)
        r.op("gpsimd", lambda e: e.dma_start(out=dbg["rows"], in_=ROWS[:]),
             inc=S_DB, inc_n=16)
        r.op("gpsimd", lambda e: e.dma_start(out=dbg["spp"], in_=SPP[:]),
             inc=S_DB, inc_n=16)
        r.op("gpsimd", lambda e: e.dma_start(out=dbg["bis"], in_=BIS[:, 0:34]),
             inc=S_DB, inc_n=16)
    r.wait("gpsimd", [(ST[0], ST[0].n), (ST[1], ST[1].n)])


def emit_attn_a(r: Rec, es, nc, x_in, x_out, ctx_scr, wdr, C, tag, dbg=None):
    LD = [r.sem("ld0"), r.sem("ld1")]
    S_LW = r.sem("lw")
    S_T = r.sem("et")
    S_V = r.sem("ev")
    S_SC = r.sem("esc")
    ST = [r.sem("st0"), r.sem("st1")]
    CL = [r.sem("yl0"), r.sem("yl1")]

    NW = 15
    NHP = 8

    sb = lambda name, shape, dt: es.enter_context(nc.sbuf_tensor(name + tag, shape, dt))
    psd = lambda name, shape, dt: es.enter_context(nc.psum_tensor(name + tag, shape, dt))

    ZB = sb("zb", [P, NC * T], BF16)
    ROWS = sb("rows", [1, 8 * TT], F32)
    RB = sb("rb", [1, TT], BF16)
    EVR = sb("evr", [P, TT], BF16)
    EVB = sb("evb", [P, TT], BF16)
    ABQ = sb("abq", [P, NC], F32)
    ABK = sb("abk", [P, NC], F32)
    ABV = sb("abv", [P, NC], F32)
    ABO = sb("abo", [P, NC], F32)

    PS_S = psd("ps_s", [P, 4 * TT], F32)
    PS_O = [psd(f"ps_o{i}", [P, 260], F32) for i in range(2)]
    PS_T = psd("ps_t", [P, TT], BF16)
    PS_X = psd("ps_x", [2, TT], F32)

    row = lambda off, n=TT: ROWS[0:1, off:off + n]
    R_M, R_MS, R_VAR, R_R, R_BN, R_T1, R_T2, R_MISC = (i * TT for i in range(8))

    x3 = lambda ap: ap.rearrange("(c p) t -> p c t", p=P)
    ZB3 = ZB[:].rearrange("p (c t) -> p c t", c=NC)
    ctx3 = ctx_scr.rearrange("(c p) t -> p c t", p=P)

    r.op("vector", lambda e: e.memset(ROWS[:], 0.0))

    _BIAS_LOADS = [("a_bq", ABQ), ("a_bk", ABK), ("a_bv", ABV), ("a_bo", ABO)]

    with ExitStack() as lns:
        lsb = lambda name, shape, dt: lns.enter_context(nc.sbuf_tensor(name + tag, shape, dt))
        XT = [lsb(f"xt{i}", [P, NC * TT], F32) for i in range(2)]
        XS = lsb("xs", [P, NC * 1024], BF16)
        XS4 = XS[:].rearrange("p (c two q) -> p c two q", two=2, q=TT)
        va = {}
        sc_done = 0
        for t in range(NT):
            buf = XT[t % 2]
            S_LD = LD[t % 2]
            buf3 = buf[:].rearrange("p (c q) -> p c q", c=NC)
            wl = [(S_V, va[t - 2])] if (t - 2) in va else []
            r.op("sync", lambda e, t=t, buf=buf: e.dma_start(
                out=buf[:].rearrange("p (c q) -> p c q", c=NC),
                in_=x3(x_in)[:, :, t * TT:(t + 1) * TT]), inc=S_LD, inc_n=16, wait=wl)
            ld_t = S_LD.n
            wcast = [(S_LD, ld_t)]
            if t > 0:
                wcast.append((S_SC, sc_done))
                wcast.append((S_T, r._aa_stats_t))
            r.op("vector", lambda e, buf3=buf3: e.tensor_copy(XS4[:, :, 0, :], buf3),
                 wait=wcast)
            xcast_v = S_V.n
            r.op("scalar", lambda e: e.activation(XS4[:, :, 1, :], XS4[:, :, 0, :],
                                                  AF.Square), inc=S_SC,
                 wait=[(S_V, xcast_v)])
            sq_s = S_SC.n
            for c in range(NC):
                r.op("tensor", lambda e, c=c: e.matmul(
                    PS_S[0:1, 0:TT], C["ones128_b"], XS4[:, c, 0, :],
                    start=(c == 0), stop=(c == NC - 1)),
                    wait=[(S_V, xcast_v)] if c == 0 else [])
            for c in range(NC):
                r.op("tensor", lambda e, c=c: e.matmul(
                    PS_S[0:1, TT:2 * TT], C["ones128_b"], XS4[:, c, 1, :],
                    start=(c == 0), stop=(c == NC - 1)),
                    inc=(S_T if c == NC - 1 else None),
                    wait=[(S_SC, sq_s)] if c == 0 else [])
            stats_t = S_T.n
            r._aa_stats_t = stats_t
            r.op("vector", lambda e: e.tensor_scalar(row(R_M), PS_S[0:1, 0:TT],
                                                     1.0 / D, None, ALU.mult),
                 wait=[(S_T, stats_t)])
            r.op("vector", lambda e: e.tensor_scalar(row(R_MS), PS_S[0:1, TT:2 * TT],
                                                     1.0 / D, None, ALU.mult))
            r.op("vector", lambda e: e.tensor_tensor(row(R_T1), row(R_M), row(R_M),
                                                     ALU.mult))
            r.op("vector", lambda e: e.tensor_tensor(row(R_VAR), row(R_MS), row(R_T1),
                                                     ALU.subtract))
            r.op("vector", lambda e: e.tensor_scalar(row(R_VAR), row(R_VAR), 1e-5,
                                                     None, ALU.add), inc=S_V)
            var_v = S_V.n
            r.op("scalar", lambda e: e.activation(row(R_T2), row(R_VAR), AF.Sqrt),
                 inc=S_SC, wait=[(S_V, var_v)])
            r.op("vector", lambda e: e.reciprocal(row(R_R), row(R_T2)),
                 wait=[(S_SC, S_SC.n)])
            r.op("vector", lambda e: e.tensor_tensor(row(R_T1), row(R_M), row(R_R),
                                                     ALU.mult))
            r.op("vector", lambda e: e.tensor_scalar(row(R_BN), row(R_T1), -1.0,
                                                     None, ALU.mult))
            r.op("vector", lambda e: e.tensor_copy(RB[:], row(R_R)), inc=S_V)
            rb_v = S_V.n
            r.op("tensor", lambda e: e.matmul(PS_S[:, 2 * TT:3 * TT], C["ones1_b"],
                                              RB[:], start=True, stop=True), inc=S_T,
                 wait=[(S_V, rb_v)])
            r.op("vector", lambda e: e.tensor_copy(EVR[:], PS_S[:, 2 * TT:3 * TT]),
                 wait=[(S_T, S_T.n)], inc=S_V)
            r.op("vector", lambda e: e.tensor_copy(RB[:], row(R_BN)), inc=S_V)
            rb2_v = S_V.n
            r.op("tensor", lambda e: e.matmul(PS_S[:, 3 * TT:4 * TT], C["ones1_b"],
                                              RB[:], start=True, stop=True), inc=S_T,
                 wait=[(S_V, rb2_v)])
            r.op("vector", lambda e: e.tensor_copy(EVB[:], PS_S[:, 3 * TT:4 * TT]),
                 wait=[(S_T, S_T.n)])
            for c in range(NC):
                r.op("vector", lambda e, c=c: e.tensor_tensor(
                    ZB3[:, c, t * TT:(t + 1) * TT]
                    if False else XS4[:, c, 1, :],
                    XS4[:, c, 0, :], EVR[:], ALU.mult))
            for c in range(NC):
                r.op("vector", lambda e, c=c, t=t: e.tensor_tensor(
                    ZB3[:, c, t * TT:(t + 1) * TT], XS4[:, c, 1, :],
                    EVB[:], ALU.add), inc=(S_V if c == NC - 1 else None))
            va[t] = S_V.n
            sc_done = S_SC.n

    with ExitStack() as hps:
        hsb = lambda name, shape, dt: hps.enter_context(nc.sbuf_tensor(name + tag, shape, dt))
        QH = [hsb("qa", [64, T], BF16), hsb("qb", [64, T], BF16)]
        KH = [hsb("ka", [64, T], BF16), hsb("kb", [64, T], BF16)]
        SQ = hsb("sq", [P, TT], BF16)
        VF = hsb("vf", [P, TT], BF16)
        VH = [hsb("va", [P, 32 * 66], BF16), hsb("vb", [P, 32 * 66], BF16)]
        VH4 = [v[:].rearrange("p (b c) -> p b c", c=66) for v in VH]
        QMX = hsb("qmx", [2, 8], F32)
        KMX = hsb("kmx", [2, 8], F32)
        CROW = hsb("crow", [2, 8], F32)
        NEGCR = hsb("negcr", [1, 2], F32)
        NEGCC = hsb("negcc", [P, 2], F32)
        PBT = [hsb("pbt0", [P, 4 * TT], BF16), hsb("pbt1", [P, 4 * TT], BF16)]
        PBST = hsb("pbst", [P, 2 * TT], F32)
        P4 = [hsb("p4a", [P, 4 * TT], BF16), hsb("p4b", [P, 4 * TT], BF16)]
        RC = hsb("rc", [P, 4], F32)
        RC2 = hsb("rc2", [P, 4], F32)
        CTXT = hsb("ctxt", [P, 32 * P], BF16)
        CTXT4 = CTXT[:].rearrange("p (b c) -> p b c", c=P)
        CTXF = hsb("ctxf", [P, T], BF16)
        W3 = [hsb("wq3", [P, NC * P], BF16), hsb("wk3", [P, NC * P], BF16),
              hsb("wv3", [P, NC * P], BF16)]
        WST2 = hsb("wst2", [P, NC * P], F32)
        CNTW = hsb("cntw", [P, 64], F32)

        for nmb, tsr in _BIAS_LOADS + [("a_cntw", CNTW)]:
            r.op("sync", lambda e, nmb=nmb, tsr=tsr: e.dma_start(
                out=tsr[:], in_=wdr[nmb]), inc=S_LW, inc_n=16,
                wait=[(S_V, S_V.n)])
        r.op("vector", lambda e: e.tensor_scalar(ABQ[:], ABQ[:], 0.125, None,
                                                 ALU.mult), inc=S_V,
             wait=[(S_LW, S_LW.n)])

        for hp in range(NHP):
            for wi, wname in enumerate(("a_Wq", "a_Wk", "a_Wv")):
                r.op("sync", lambda e, wname=wname, hp=hp: e.dma_start(
                    out=WST2[:].rearrange("p (c j) -> p c j", j=P),
                    in_=wdr[wname].rearrange("(c p) d -> p c d", p=P)
                    [:, :, hp * P:(hp + 1) * P]),
                    inc=S_LW, inc_n=16, wait=[(S_V, S_V.n)])
                r.op("vector", lambda e, wi=wi: e.tensor_copy(W3[wi][:], WST2[:]),
                     wait=[(S_LW, S_LW.n)], inc=S_V)
            for h in range(2):
                for half in range(2):
                    r.op("sync", lambda e, hp=hp, h=h, half=half: e.dma_start(
                        out=PBST[:].rearrange("p (k q) -> p k q", k=2),
                        in_=wdr["a_pbT"][2 * hp + h]
                        .rearrange("(kb p) q -> p kb q", p=P)[:, 2 * half:2 * half + 2, :]),
                        inc=S_LW, inc_n=16, wait=[(S_V, S_V.n)])
                    r.op("vector", lambda e, h=h, half=half: e.tensor_copy(
                        PBT[h][:, half * 2 * TT:(half + 1) * 2 * TT], PBST[:]),
                        wait=[(S_LW, S_LW.n)], inc=S_V)
            pbt_v = S_V.n

            hpq = [(ST[hp % 2], ST[hp % 2].n)] if hp >= 2 else []
            r.op("vector", lambda e: e.memset(VH4[0][:, :, 64:65], 1.0),
                 wait=[(S_T, S_T.n)] + hpq)
            r.op("vector", lambda e: e.memset(VH4[1][:, :, 64:65], 1.0))
            r.op("vector", lambda e: e.memset(CTXT[:], 0.0), inc=S_V)

            m = 0
            ev_done = {}
            for t in range(NT):
                for wi in range(3):
                    pc = (m % 2) * TT
                    wprev = [(S_SC, ev_done[m - 2])] if (m - 2) in ev_done else []
                    if m < 3:
                        wprev = wprev + [(S_V, pbt_v)]
                    for c in range(NC):
                        r.op("tensor", lambda e, wi=wi, c=c, t=t, pc=pc: e.matmul(
                            PS_S[:, pc:pc + TT], W3[wi][:, c * P:(c + 1) * P],
                            ZB3[:, c, t * TT:(t + 1) * TT],
                            start=(c == 0), stop=(c == NC - 1)),
                            inc=(S_T if c == NC - 1 else None),
                            wait=(wprev if c == 0 else []))
                    mm_t = S_T.n
                    if wi == 0:
                        r.op("scalar", lambda e, t=t, pc=pc, hp=hp: e.activation(
                            QH[0][:, t * TT:(t + 1) * TT], PS_S[0:64, pc:pc + TT],
                            AF.Identity, bias=ABQ[0:64, hp:hp + 1], scale=0.125),
                            wait=[(S_T, mm_t)])
                        r.op("scalar", lambda e, t=t, pc=pc, hp=hp: e.activation(
                            QH[1][:, t * TT:(t + 1) * TT], PS_S[64:P, pc:pc + TT],
                            AF.Identity, bias=ABQ[64:P, hp:hp + 1], scale=0.125))
                        r.op("scalar", lambda e, pc=pc, hp=hp: e.activation(
                            SQ[:], PS_S[:, pc:pc + TT], AF.Square,
                            bias=ABQ[:, hp:hp + 1], scale=0.125), inc=S_SC)
                        ev_done[m] = S_SC.n
                        r.op("tensor", lambda e: e.matmul(
                            PS_X[:, 0:TT], C["ones2h"], SQ[:], start=True, stop=True),
                            inc=S_T, wait=[(S_SC, S_SC.n), (S_V, S_V.n)])
                        r.op("vector", lambda e, t=t: e.tensor_reduce(
                            QMX[0:2, t:t + 1], PS_X[0:2, 0:TT], AX.X, ALU.max),
                            wait=[(S_T, S_T.n)], inc=S_V)
                    elif wi == 1:
                        r.op("scalar", lambda e, t=t, pc=pc, hp=hp: e.activation(
                            KH[0][:, t * TT:(t + 1) * TT], PS_S[0:64, pc:pc + TT],
                            AF.Identity, bias=ABK[0:64, hp:hp + 1]),
                            wait=[(S_T, mm_t)])
                        r.op("scalar", lambda e, t=t, pc=pc, hp=hp: e.activation(
                            KH[1][:, t * TT:(t + 1) * TT], PS_S[64:P, pc:pc + TT],
                            AF.Identity, bias=ABK[64:P, hp:hp + 1]))
                        r.op("scalar", lambda e, pc=pc, hp=hp: e.activation(
                            SQ[:], PS_S[:, pc:pc + TT], AF.Square,
                            bias=ABK[:, hp:hp + 1]), inc=S_SC)
                        ev_done[m] = S_SC.n
                        r.op("tensor", lambda e: e.matmul(
                            PS_X[:, 0:TT], C["ones2h"], SQ[:], start=True, stop=True),
                            inc=S_T, wait=[(S_SC, S_SC.n), (S_V, S_V.n)])
                        r.op("vector", lambda e, t=t: e.tensor_reduce(
                            KMX[0:2, t:t + 1], PS_X[0:2, 0:TT], AX.X, ALU.max),
                            wait=[(S_T, S_T.n)], inc=S_V)
                    else:
                        r.op("scalar", lambda e, pc=pc, hp=hp: e.activation(
                            VF[:], PS_S[:, pc:pc + TT], AF.Identity,
                            bias=ABV[:, hp:hp + 1]), inc=S_SC,
                            wait=[(S_T, mm_t), (S_V, S_V.n)])
                        ev_done[m] = S_SC.n
                        for tb in range(4):
                            r.op("tensor", lambda e, tb=tb: e.transpose(
                                PS_T[:, tb * P:(tb + 1) * P], VF[:, tb * P:(tb + 1) * P],
                                C["ident_b"]),
                                inc=(S_T if tb == 3 else None),
                                wait=[(S_SC, S_SC.n), (S_V, S_V.n)] if tb == 0 else [])
                        PS_T3 = PS_T[:].rearrange("p (b c) -> p b c", c=P)
                        r.op("vector", lambda e, t=t: e.tensor_copy(
                            VH4[0][:, 4 * t:4 * t + 4, 0:64], PS_T3[:, :, 0:64]),
                            wait=[(S_T, S_T.n)])
                        r.op("vector", lambda e, t=t: e.tensor_copy(
                            VH4[1][:, 4 * t:4 * t + 4, 0:64], PS_T3[:, :, 64:P]),
                            inc=S_V)
                    m += 1
            prep_sc = S_SC.n

            r.op("vector", lambda e: e.tensor_reduce(CROW[0:2, 0:1], QMX[:], AX.X,
                                                     ALU.max))
            r.op("vector", lambda e: e.tensor_reduce(CROW[0:2, 1:2], KMX[:], AX.X,
                                                     ALU.max), inc=S_V)
            r.op("scalar", lambda e: e.activation(CROW[0:2, 2:3], CROW[0:2, 0:1],
                                                  AF.Sqrt), wait=[(S_V, S_V.n)])
            r.op("scalar", lambda e: e.activation(CROW[0:2, 3:4], CROW[0:2, 1:2],
                                                  AF.Sqrt), inc=S_SC)
            r.op("vector", lambda e: e.tensor_tensor(CROW[0:2, 4:5], CROW[0:2, 2:3],
                                                     CROW[0:2, 3:4], ALU.mult),
                 wait=[(S_SC, S_SC.n)])
            r.op("vector", lambda e: e.tensor_scalar(CROW[0:2, 5:6], CROW[0:2, 4:5],
                                                     -1.0, None, ALU.mult), inc=S_V)
            r.op("tensor", lambda e: e.matmul(PS_O[0][0:1, 0:2], CROW[0:2, 5:6],
                                              C["ident_f"][0:2, 0:2], start=True,
                                              stop=True, is_transpose=True),
                 inc=S_T, wait=[(S_V, S_V.n)])
            r.op("vector", lambda e: e.tensor_copy(NEGCR[:], PS_O[0][0:1, 0:2]),
                 wait=[(S_T, S_T.n)], inc=S_V)
            for h in range(2):
                r.op("tensor", lambda e, h=h: e.matmul(
                    PS_O[0][:, 4 + h:5 + h], C["ones1_f"], NEGCR[0:1, h:h + 1],
                    start=True, stop=True), inc=S_T, wait=[(S_V, S_V.n)])
                r.op("vector", lambda e, h=h: e.tensor_copy(
                    NEGCC[:, h:h + 1], PS_O[0][:, 4 + h:5 + h]),
                    wait=[(S_T, S_T.n)], inc=S_V)
            negcc_v = S_V.n

            exp_s = {}
            pv_t = {}
            stt_v = {}

            def emit_stage(s):
                g, i = divmod(s, 2)
                w, h = divmod(g, 2)
                for j in range(2):
                    kb = 2 * i + j
                    pcol = i * 1024 + j * TT
                    w0 = []
                    if j == 0:
                        if s - 2 in exp_s:
                            w0.append((S_SC, exp_s[s - 2]))
                        if s == 0:
                            w0 += [(S_SC, prep_sc), (S_V, pbt_v)]
                    r.op("tensor", lambda e, h=h, kb=kb, pcol=pcol: e.matmul(
                        PS_S[:, pcol:pcol + TT], C["ident_b"],
                        PBT[h][:, kb * TT:(kb + 1) * TT], start=True, stop=False),
                        wait=w0)
                    r.op("tensor", lambda e, h=h, w=w, kb=kb, pcol=pcol: e.matmul(
                        PS_S[:, pcol:pcol + TT],
                        KH[h][0:64, 256 * w + 128 * kb:256 * w + 128 * kb + 128],
                        QH[h][0:64, 256 * w:256 * w + TT],
                        start=False, stop=True),
                        inc=(S_T if j == 1 else None))

            def emit_exp(s):
                g, i = divmod(s, 2)
                w, h = divmod(g, 2)
                wl = [(S_T, r._aa_mm_s[s])]
                if s < 2:
                    wl.append((S_V, negcc_v))
                if g - 2 in pv_t:
                    wl.append((S_T, pv_t[g - 2]))
                r.op("scalar", lambda e, g=g, i=i, h=h: e.activation(
                    P4[g % 2][:, i * 1024:(i + 1) * 1024],
                    PS_S[:, i * 1024:(i + 1) * 1024], AF.Exp,
                    bias=NEGCC[:, h:h + 1]), inc=S_SC, wait=wl)
                exp_s[s] = S_SC.n

            def emit_pv(g):
                w, h = divmod(g, 2)
                for qb in range(4):
                    for kb in range(4):
                        w0 = []
                        if qb == 0 and kb == 0:
                            w0.append((S_SC, exp_s[2 * g + 1]))
                            if g == 0:
                                w0.append((S_V, negcc_v))
                            if g - 2 in stt_v:
                                w0.append((S_V, stt_v[g - 2]))
                        r.op("tensor", lambda e, g=g, h=h, w=w, qb=qb, kb=kb: e.matmul(
                            PS_O[g % 2][:, 65 * qb:65 * qb + 65],
                            P4[g % 2][:, kb * TT + qb * P:kb * TT + (qb + 1) * P],
                            VH4[h][:, 2 * w + kb, 0:65],
                            start=(kb == 0), stop=(kb == 3)),
                            inc=(S_T if qb == 3 and kb == 3 else None), wait=w0)
                pv_t[g] = S_T.n

            def emit_post(g):
                w, h = divmod(g, 2)
                cb = 32 if w == NW - 1 else 0
                r.op("vector", lambda e, g=g: e.reciprocal(
                    RC[:], PS_O[g % 2][:].rearrange("p (qb c) -> p qb c", c=65)
                    [:, :, 64]), wait=[(S_T, pv_t[g])])
                r.op("vector", lambda e, w=w, cb=cb: e.tensor_tensor(
                    RC2[:], RC[:], CNTW[:, cb + 2 * w:cb + 2 * w + 4], ALU.mult))
                for qb in range(4):
                    r.op("vector", lambda e, g=g, w=w, h=h, qb=qb: e.scalar_tensor_tensor(
                        CTXT4[:, 2 * w + qb, 64 * h:64 * h + 64],
                        PS_O[g % 2][:, 65 * qb:65 * qb + 64], RC2[:, qb:qb + 1],
                        CTXT4[:, 2 * w + qb, 64 * h:64 * h + 64],
                        ALU.mult, ALU.add),
                        inc=(S_V if qb == 3 else None))
                stt_v[g] = S_V.n

            r._aa_mm_s = {}
            NG = NW * 2
            for g in range(NG):
                emit_stage(2 * g)
                r._aa_mm_s[2 * g] = S_T.n
                emit_stage(2 * g + 1)
                r._aa_mm_s[2 * g + 1] = S_T.n
                emit_exp(2 * g)
                emit_exp(2 * g + 1)
                if g > 0:
                    emit_pv(g - 1)
                    emit_post(g - 1)
            emit_pv(NG - 1)
            emit_post(NG - 1)

            ctw = [(ST[(hp - 1) % 2], ST[(hp - 1) % 2].n)] if hp >= 1 else []
            for q4 in range(8):
                for j in range(4):
                    w0 = []
                    if q4 == 0 and j == 0:
                        w0 = [(S_V, S_V.n)]
                    if j == 0 and (q4 > 0 or hp > 0):
                        w0.append((S_SC, S_SC.n))
                    r.op("tensor", lambda e, q4=q4, j=j: e.transpose(
                        PS_T[:, j * P:(j + 1) * P], CTXT[:, (4 * q4 + j) * P:(4 * q4 + j + 1) * P],
                        C["ident_b"]), inc=(S_T if j == 3 else None), wait=w0)
                wev = [(S_T, S_T.n)]
                if q4 == 0:
                    wev += ctw
                r.op("scalar", lambda e, q4=q4: e.activation(
                    CTXF[:, q4 * TT:(q4 + 1) * TT], PS_T[:], AF.Copy), inc=S_SC,
                    wait=wev)
            r.op("gpsimd", lambda e, hp=hp: e.dma_start(
                out=ctx_scr[hp * P:(hp + 1) * P, :], in_=CTXF[:]),
                inc=ST[hp % 2], inc_n=16, wait=[(S_SC, S_SC.n)])

    with ExitStack() as fps:
        fsb = lambda name, shape, dt: fps.enter_context(nc.sbuf_tensor(name + tag, shape, dt))
        XT2 = [fsb(f"xo{i}", [P, NC * TT], F32) for i in range(2)]
        CIN = [fsb(f"ci{i}", [P, NC * TT], BF16) for i in range(2)]
        WOB = fsb("wob", [P, NC * 1024], BF16)
        WOST = fsb("wost", [P, 1024], F32)
        OUTSB = fsb("outsb", [P, NC * TT], BF16)

        for c in range(NC):
            wq = [(S_V, S_V.n)]
            if c == 0:
                wq += [(S_T, S_T.n), (S_SC, S_SC.n),
                       (ST[0], ST[0].n), (ST[1], ST[1].n)]
            r.op("sync", lambda e, c=c: e.dma_start(
                out=WOST[:], in_=wdr["a_Wo"][c * P:(c + 1) * P, :]),
                inc=S_LW, inc_n=16, wait=wq)
            r.op("vector", lambda e, c=c: e.tensor_copy(
                WOB[:, c * 1024:(c + 1) * 1024], WOST[:]), inc=S_V,
                wait=[(S_LW, S_LW.n)])
        wo_v = S_V.n

        ev_j = {}
        for t in range(NT):
            buf = XT2[t % 2]
            cin = CIN[t % 2]
            cin3 = cin[:].rearrange("p (c q) -> p c q", c=NC)
            S_LD = LD[t % 2]
            S_CL = CL[t % 2]
            S_ST = ST[t % 2]
            xwl = [(S_V, S_V.n)] if t < 2 else [(S_ST, S_ST.n)]
            r.op("sync", lambda e, t=t, buf=buf: e.dma_start(
                out=buf[:].rearrange("p (c q) -> p c q", c=NC),
                in_=x3(x_in)[:, :, t * TT:(t + 1) * TT]), inc=S_LD, inc_n=16,
                wait=xwl)
            ld_t = S_LD.n
            cwl = [(ST[t % 2], ST[t % 2].n)] if t < 2 else [(S_V, S_V.n)]
            r.op("sync", lambda e, t=t, cin=cin: e.dma_start(
                out=cin[:].rearrange("p (c q) -> p c q", c=NC),
                in_=ctx3[:, :, t * TT:(t + 1) * TT]), inc=S_CL, inc_n=16,
                wait=cwl)
            cl_t = S_CL.n
            for j in range(NC):
                pc = (j % 2) * TT
                w0 = []
                if j < 2:
                    w0.append((S_CL, cl_t))
                    if t == 0 and j == 0:
                        w0.append((S_V, wo_v))
                if (t, j - 2) in ev_j:
                    w0.append((S_SC, ev_j[(t, j - 2)]))
                elif (t - 1, j + 6) in ev_j:
                    w0.append((S_SC, ev_j[(t - 1, j + 6)]))
                for c in range(NC):
                    r.op("tensor", lambda e, j=j, c=c, pc=pc, cin3=cin3: e.matmul(
                        PS_S[:, pc:pc + TT], WOB[:, c * 1024 + j * P:c * 1024 + (j + 1) * P],
                        cin3[:, c, :], start=(c == 0), stop=(c == NC - 1)),
                        inc=(S_T if c == NC - 1 else None),
                        wait=(w0 if c == 0 else []))
                r.op("scalar", lambda e, j=j, pc=pc: e.activation(
                    OUTSB[:, j * TT:(j + 1) * TT], PS_S[:, pc:pc + TT], AF.Identity,
                    bias=ABO[:, j:j + 1]), inc=S_SC, wait=[(S_T, S_T.n)])
                ev_j[(t, j)] = S_SC.n
            for j in range(NC):
                r.op("vector", lambda e, j=j, buf=buf: e.tensor_tensor(
                    buf[:, j * TT:(j + 1) * TT], buf[:, j * TT:(j + 1) * TT],
                    OUTSB[:, j * TT:(j + 1) * TT], ALU.add),
                    wait=[(S_SC, ev_j[(t, j)])] + ([(S_LD, ld_t)] if j == 0 else []),
                    inc=(S_V if j == NC - 1 else None))
            out_v = S_V.n
            r.op("gpsimd", lambda e, t=t, buf=buf: e.dma_start(
                out=x3(x_out)[:, :, t * TT:(t + 1) * TT],
                in_=buf[:].rearrange("p (c q) -> p c q", c=NC)),
                inc=S_ST, inc_n=16, wait=[(S_V, out_v)])

    r.wait("gpsimd", [(ST[0], ST[0].n), (ST[1], ST[1].n)])


def emit_attn_b(r: Rec, es, nc, x_in, x_out, wdr, scr, rscr, C, tag, dbg=None):
    TB = 256
    NTB = (T // TB) * (0 if os.environ.get('AB_SKIP_L1') else 1)
    NCH = TB // 64
    H = 4
    DV = 256
    SWD = 512
    STW = 1024
    NST = (T // STW) * (0 if os.environ.get('AB_SKIP_L2') else 1)
    NST = int(os.environ.get('AB_NST', NST))
    NSP = STW // SWD
    NSP = int(os.environ.get('AB_NSP', NSP))

    LD = [r.sem("ld0"), r.sem("ld1")]
    SWS = [r.sem("w0"), r.sem("w1")]
    RD = [r.sem("r0"), r.sem("r1")]
    S_LW = r.sem("lw")
    S_T = r.sem("et")
    S_V = r.sem("ev")
    S_SC = r.sem("esc")
    ST = [r.sem("st0"), r.sem("st1")]

    sb = lambda name, shape, dt: es.enter_context(nc.sbuf_tensor(name + tag, shape, dt))
    psd = lambda name, shape, dt: es.enter_context(nc.psum_tensor(name + tag, shape, dt))

    W4 = sb("w4", [64, 1024], BF16)
    W4OG = sb("w4og", [32, 512], BF16)
    WG1S = sb("wg1s", [16, 1024], BF16)
    WOG1S = sb("wog1s", [NC, 512], BF16)
    WBB = sb("wbb", [P, NC * 4], BF16)
    WRESB = sb("wresb", [P, NC * 4], BF16)
    WG2B = sb("wg2b", [P, NC * 4], BF16)
    WOG2B = sb("wog2b", [P, 4], BF16)
    WCV = [sb(f"wcv{i}", [P, NC * 4], F32) for i in range(3)]
    FIRL = sb("firl", [P, NC * 64], F32)
    FIRS = sb("firs", [P, NC * 5], F32)
    NW = sb("nw", [P, 2], F32)
    BG1 = sb("bg1", [P, NC], F32)
    BOG1 = sb("bog1", [P, 4], F32)
    BCOL = sb("bcol", [4, 8], F32)
    LT = sb("lt", [1, 8], F32)
    RSP4 = sb("rsp4", [4, 2], F32)
    WST = sb("wstb", [P, 1024], F32)

    PB_A = psd("pb_a", [P, TT], F32)
    PB_B = psd("pb_b", [P, TT], F32)
    PB_C = psd("pb_c", [P, TT], F32)
    PB_D = psd("pb_d", [P, TT], F32)
    PB_T = psd("pb_t", [P, TT], BF16)
    PB_R1 = psd("pb_r1", [4, TT], F32)
    PB_R2 = psd("pb_r2", [4, TT], F32)
    PB_R3 = psd("pb_r3", [4, TT], F32)

    x3 = lambda ap: ap.rearrange("(c p) t -> p c t", p=P)
    vin3 = rscr["vin"].rearrange("(c p) t -> p c t", p=P)
    z3d = rscr["z"].rearrange("(c p) t -> p c t", p=P)
    dl3 = rscr["dl"].rearrange("(c p) t -> p c t", p=P)

    E4S = C["e4s"]
    HSEL = C["headsel"]
    MSL4 = C["msl4"]
    MSU4 = C["msu4"]
    MUI4 = C["mui4"]
    ID4 = C["id4"]
    IDB = C["ident_b"]

    def tile_cols(t):
        return slice(t * TB, (t + 1) * TB)

    def loadw(dst, name):
        r.op("sync", lambda e, dst=dst, name=name: e.dma_start(
            out=dst[:], in_=wdr[name]), inc=S_LW, inc_n=16, wait=[(S_V, S_V.n)])
        r.wait("vector", [(S_LW, S_LW.n)])
        r.op("vector", lambda e: e.engine_nop(), inc=S_V)

    for nm, dst in (("b_wqc", WCV[0]), ("b_wkc", WCV[1]), ("b_wvc", WCV[2]),
                    ("b_firl", FIRL), ("b_firs", FIRS), ("b_nw", NW),
                    ("b_bg1", BG1), ("b_bog1", BOG1)):
        loadw(dst, nm)
    for nm, dst in (("b_Wb", WBB), ("b_Wres", WRESB), ("b_Wg2", WG2B)):
        r.op("sync", lambda e, nm=nm: e.dma_start(
            out=WST[:, 0:NC * 4], in_=wdr[nm]), inc=S_LW, inc_n=16,
            wait=[(S_V, S_V.n)])
        r.op("vector", lambda e, dst=dst: e.tensor_copy(dst[:], WST[:, 0:NC * 4]),
             inc=S_V, wait=[(S_LW, S_LW.n)])
    r.op("sync", lambda e: e.dma_start(out=WST[:, 0:4], in_=wdr["b_Wog2"]),
         inc=S_LW, inc_n=16, wait=[(S_V, S_V.n)])
    r.op("vector", lambda e: e.tensor_copy(WOG2B[:], WST[:, 0:4]), inc=S_V,
         wait=[(S_LW, S_LW.n)])
    for i, nm in enumerate(("b_bg2", "b_bres", "b_crl")):
        r.op("sync", lambda e, nm=nm, i=i: e.dma_start(
            out=BCOL[0:4, i:i + 1], in_=wdr[nm]), inc=S_LW, inc_n=16,
            wait=[(S_V, S_V.n)])
        r.wait("vector", [(S_LW, S_LW.n)])
        r.op("vector", lambda e: e.engine_nop(), inc=S_V)
    r.op("sync", lambda e: e.dma_start(out=LT[0:1, 0:1], in_=wdr["b_lt"]),
         inc=S_LW, inc_n=16, wait=[(S_V, S_V.n)])
    r.op("sync", lambda e: e.dma_start(out=LT[0:1, 4:5], in_=wdr["b_bog2"]),
         inc=S_LW, inc_n=16, wait=[(S_LW, S_LW.n)])
    r.wait("scalar", [(S_LW, S_LW.n)])
    r.op("scalar", lambda e: e.activation(LT[0:1, 1:2], LT[0:1, 0:1], AF.Abs))
    r.op("scalar", lambda e: e.activation(LT[0:1, 1:2], LT[0:1, 1:2], AF.Exp,
                                          scale=-1.0), inc=S_SC)
    r.op("vector", lambda e: e.tensor_scalar(LT[0:1, 1:2], LT[0:1, 1:2], 1.0,
                                             None, ALU.add),
         wait=[(S_SC, S_SC.n)], inc=S_V)
    r.op("scalar", lambda e: e.activation(LT[0:1, 1:2], LT[0:1, 1:2], AF.Ln),
         inc=S_SC, wait=[(S_V, S_V.n)])
    r.op("vector", lambda e: e.tensor_scalar(LT[0:1, 2:3], LT[0:1, 0:1], 0.0,
                                             None, ALU.max),
         wait=[(S_SC, S_SC.n)])
    r.op("vector", lambda e: e.tensor_tensor(LT[0:1, 2:3], LT[0:1, 2:3],
                                             LT[0:1, 1:2], ALU.add))
    r.op("vector", lambda e: e.reciprocal(LT[0:1, 3:4], LT[0:1, 2:3]), inc=S_V)
    r.op("tensor", lambda e: e.matmul(PB_R1[0:4, 0:1], C["ones1_f"][0:1, 0:4],
                                      LT[0:1, 3:4], start=True, stop=True),
         inc=S_T, wait=[(S_V, S_V.n)])
    r.op("vector", lambda e: e.tensor_copy(RSP4[0:4, 0:1], PB_R1[0:4, 0:1]),
         wait=[(S_T, S_T.n)])
    r.op("vector", lambda e: e.tensor_tensor(RSP4[0:4, 1:2], BCOL[0:4, 0:1],
                                             RSP4[0:4, 0:1], ALU.mult), inc=S_V)
    r.op("scalar", lambda e: e.activation(BCOL[0:4, 3:4], BCOL[0:4, 2:3],
                                          AF.Sigmoid), inc=S_SC)

    r.op("sync", lambda e: e.dma_start(out=WST[0:16, :], in_=wdr["b_Wg1s"]),
         inc=S_LW, inc_n=16, wait=[(S_V, S_V.n)])
    r.op("vector", lambda e: e.tensor_scalar(WG1S[:], WST[0:16, :], 0.25, None,
                                             ALU.mult), inc=S_V,
         wait=[(S_LW, S_LW.n)])
    r.op("sync", lambda e: e.dma_start(out=WST[0:8, 0:512], in_=wdr["b_Wog1s"]),
         inc=S_LW, inc_n=16, wait=[(S_V, S_V.n)])
    r.op("vector", lambda e: e.tensor_scalar(WOG1S[:], WST[0:8, 0:512], 0.25,
                                             None, ALU.mult), inc=S_V,
         wait=[(S_LW, S_LW.n)])
    w4v = S_V.n
    for h in range(4):
        r.op("sync", lambda e, h=h: e.dma_start(
            out=rscr["w4"].rearrange("(f h) d -> h f d", h=4)[h], in_=WG1S[:]),
            inc=S_LW, inc_n=16, wait=[(S_V, w4v), (S_LW, S_LW.n)])
        r.op("sync", lambda e, h=h: e.dma_start(
            out=rscr["w4og"].rearrange("(f h) d -> h f d", h=4)[h],
            in_=WOG1S[:]), inc=S_LW, inc_n=16, wait=[(S_LW, S_LW.n)])
    r.op("sync", lambda e: e.dma_start(out=W4[:], in_=rscr["w4"]),
         inc=S_LW, inc_n=16, wait=[(S_LW, S_LW.n)])
    r.op("sync", lambda e: e.dma_start(out=W4OG[:], in_=rscr["w4og"]),
         inc=S_LW, inc_n=16, wait=[(S_LW, S_LW.n)])
    r.wait("tensor", [(S_LW, S_LW.n)])

    with ExitStack() as l1s:
        sb1 = lambda name, shape, dt: l1s.enter_context(
            nc.sbuf_tensor(name + tag, shape, dt))
        WS = [sb1(f"ws{i}", [P, NC * 1024], BF16) for i in range(2)]
        XT = sb1("xt", [P, NC * TB], F32)
        ZT = sb1("zt", [P, NC * TB], BF16)
        PRE = sb1("pre", [P, NC * (TB + 4)], BF16)
        CONVT = sb1("convt", [P, NC * TB], BF16)
        SQT = sb1("sqt", [P, NC * TB], BF16)
        QI = sb1("qi", [P, NC * TB], BF16)
        KI = sb1("ki", [P, NC * TB], BF16)
        VIH = sb1("vih", [P, NC * (TB + 64)], BF16)
        KBN = sb1("kbn", [P, NC * TB], BF16)
        VBT = sb1("vbt", [P, NC * TB], BF16)
        HIST = [sb1(f"hist{i}", [P, NC * 4], BF16) for i in range(3)]
        VTM = sb1("vtm", [64, NCH * 1024], BF16)
        KTM = sb1("ktm", [64, NCH * 1024], BF16)
        DTM = sb1("dtm", [64, NCH * 1024], BF16)
        DELTA = PRE
        SST = sb1("sst", [P, NC * DV], F32)
        SSB = sb1("ssb", [P, NC * DV], BF16)
        TTM = sb1("ttm", [64, H * 64], BF16)
        P2M = sb1("p2m", [64, H * 64], BF16)
        P2TM = sb1("p2tm", [64, H * 64], BF16)
        U2 = sb1("u2", [64, DV], BF16)
        YB = sb1("yb", [64, DV], BF16)
        ATN = sb1("atn", [64, H * 64], BF16)
        ROWS4 = sb1("rows4", [4, 8 * TB], F32)
        ROWS = ROWS4
        RB4 = sb1("rb4", [4, 4 * TB], BF16)
        RB = sb1("rbb", [1, TB], BF16)
        EVR = sb1("evrb", [P, TB], BF16)
        EVB2 = sb1("evbb", [P, TB], BF16)

        ZT3 = ZT[:].rearrange("p (c q) -> p c q", c=NC)
        XT3 = XT[:].rearrange("p (c q) -> p c q", c=NC)
        PRE3 = PRE[:].rearrange("p (c q) -> p c q", c=NC)
        VIH3 = VIH[:].rearrange("p (c q) -> p c q", c=NC)
        CON3 = CONVT[:].rearrange("p (c q) -> p c q", c=NC)
        SQ3 = SQT[:].rearrange("p (c q) -> p c q", c=NC)
        QI3 = QI[:].rearrange("p (c q) -> p c q", c=NC)
        KI3 = KI[:].rearrange("p (c q) -> p c q", c=NC)
        KBN3 = KBN[:].rearrange("p (c q) -> p c q", c=NC)
        VBT3 = VBT[:].rearrange("p (c q) -> p c q", c=NC)
        DEL3 = DELTA[:][:, 0:NC * TB].rearrange("p (c q) -> p c q", c=NC)

        row = lambda off, n=TB: ROWS[0:1, off:off + n]
        R_M, R_MS, R_VAR, R_R, R_BN, R_T1, R_T2, R_MISC = (i * TB for i in range(8))

        r.op("vector", lambda e: e.memset(SST[:], 0.0))
        r.op("vector", lambda e: e.memset(SSB[:], 0.0))
        r.op("vector", lambda e: e.memset(HIST[0][:], 0.0))
        r.op("vector", lambda e: e.memset(HIST[1][:], 0.0))
        r.op("vector", lambda e: e.memset(HIST[2][:], 0.0))
        r.op("vector", lambda e: e.memset(ROWS[:], 0.0), inc=S_V)

        for wi, (nm, ncol) in enumerate((("b_Wq", 1024), ("b_Wk", 1024),
                                         ("b_Wv", 1024), ("b_Wg1z", 1024),
                                         ("b_Wo", 1024), ("b_Wog1z", 512))):
            for c in range(NC):
                r.op("sync", lambda e, nm=nm, c=c, ncol=ncol: e.dma_start(
                    out=WST[:, 0:ncol], in_=wdr[nm][c * P:(c + 1) * P, :]),
                    inc=S_LW, inc_n=16, wait=[(S_V, S_V.n), (S_LW, S_LW.n)])
                r.op("vector", lambda e, c=c, ncol=ncol: e.tensor_copy(
                    WS[0][:, c * ncol:(c + 1) * ncol], WST[:, 0:ncol]),
                    inc=S_V,
                    wait=[(S_LW, S_LW.n)] + ([(ST[0], ST[0].n)] if c == 0 else []))
            r.op("gpsimd", lambda e, wi=wi, ncol=ncol: e.dma_start(
                out=scr[wi].rearrange("(c p) d -> p c d", p=P),
                in_=WS[0][:, 0:NC * ncol].rearrange("p (c d) -> p c d", c=NC)),
                inc=ST[0], inc_n=16, wait=[(S_V, S_V.n)])
        r.wait("sync", [(ST[0], ST[0].n)])

        scr3 = [s.rearrange("(c p) d -> p c d", p=P) for s in scr[0:3]]

        prev_ws_mm = {}
        for t in range(NTB):
            S_LD = LD[t % 2]
            ws_par = {wi: (3 * t + wi) % 2 for wi in range(3)}

            def emit_wstream(wi):
                par = ws_par[wi]
                wwl = [(S_T, prev_ws_mm[par])] if par in prev_ws_mm else [(ST[0], ST[0].n)]
                r.op("sync", lambda e, wi=wi, par=par: e.dma_start(
                    out=WS[par][:].rearrange("p (c d) -> p c d", c=NC),
                    in_=scr3[wi]), inc=SWS[par], inc_n=16, wait=wwl)
            emit_wstream(0)
            emit_wstream(1)

            xwl = [(S_V, S_V.n)] if t == 0 else [(ST[(t - 1) % 2], ST[(t - 1) % 2].n)]
            r.op("sync", lambda e, t=t: e.dma_start(
                out=XT3, in_=x3(x_in)[:, :, tile_cols(t)]), inc=S_LD, inc_n=16,
                wait=xwl)
            ld_t = S_LD.n
            wcast = [(S_LD, ld_t), (S_T, S_T.n), (S_SC, S_SC.n)]
            r.op("vector", lambda e: e.tensor_copy(CON3, XT3), wait=wcast,
                 inc=S_V)
            xcast_v = S_V.n
            r.op("scalar", lambda e: e.activation(SQ3, CON3, AF.Square), inc=S_SC,
                 wait=[(S_V, xcast_v)])
            sq_s = S_SC.n
            for c in range(NC):
                r.op("tensor", lambda e, c=c: e.matmul(
                    PB_R1[0:1, 0:TB], C["ones128_b"], CON3[:, c, :],
                    start=(c == 0), stop=(c == NC - 1)),
                    wait=[(S_V, xcast_v)] if c == 0 else [])
            for c in range(NC):
                r.op("tensor", lambda e, c=c: e.matmul(
                    PB_R2[0:1, 0:TB], C["ones128_b"], SQ3[:, c, :],
                    start=(c == 0), stop=(c == NC - 1)),
                    inc=(S_T if c == NC - 1 else None),
                    wait=[(S_SC, sq_s)] if c == 0 else [])
            stats_t = S_T.n
            r.op("vector", lambda e: e.tensor_scalar(row(R_M), PB_R1[0:1, 0:TB],
                                                     1.0 / D, None, ALU.mult),
                 wait=[(S_T, stats_t)])
            r.op("vector", lambda e: e.tensor_scalar(row(R_MS), PB_R2[0:1, 0:TB],
                                                     1.0 / D, None, ALU.mult))
            r.op("vector", lambda e: e.tensor_tensor(row(R_T1), row(R_M), row(R_M),
                                                     ALU.mult))
            r.op("vector", lambda e: e.tensor_tensor(row(R_VAR), row(R_MS), row(R_T1),
                                                     ALU.subtract))
            r.op("vector", lambda e: e.tensor_scalar(row(R_VAR), row(R_VAR), 1e-5,
                                                     None, ALU.add), inc=S_V)
            r.op("scalar", lambda e: e.activation(row(R_T2), row(R_VAR), AF.Sqrt),
                 inc=S_SC, wait=[(S_V, S_V.n)])
            r.op("vector", lambda e: e.reciprocal(row(R_R), row(R_T2)),
                 wait=[(S_SC, S_SC.n)])
            r.op("vector", lambda e: e.tensor_tensor(row(R_T1), row(R_M), row(R_R),
                                                     ALU.mult))
            r.op("vector", lambda e: e.tensor_scalar(row(R_BN), row(R_T1), -1.0,
                                                     None, ALU.mult))
            r.op("vector", lambda e: e.tensor_copy(RB[:], row(R_R)), inc=S_V)
            r.op("tensor", lambda e: e.matmul(PB_A[:, 0:TB], C["ones1_b"], RB[:],
                                              start=True, stop=True), inc=S_T,
                 wait=[(S_V, S_V.n)])
            r.op("vector", lambda e: e.tensor_copy(EVR[:], PB_A[:, 0:TB]),
                 wait=[(S_T, S_T.n)], inc=S_V)
            r.op("vector", lambda e: e.tensor_copy(RB[:], row(R_BN)), inc=S_V)
            r.op("tensor", lambda e: e.matmul(PB_B[:, 0:TB], C["ones1_b"], RB[:],
                                              start=True, stop=True), inc=S_T,
                 wait=[(S_V, S_V.n)])
            r.op("vector", lambda e: e.tensor_copy(EVB2[:], PB_B[:, 0:TB]),
                 wait=[(S_T, S_T.n)])
            for c in range(NC):
                r.op("vector", lambda e, c=c: e.tensor_tensor(
                    SQ3[:, c, :], CON3[:, c, :], EVR[:], ALU.mult))
            for c in range(NC):
                r.op("vector", lambda e, c=c: e.tensor_tensor(
                    ZT3[:, c, :], SQ3[:, c, :], EVB2[:], ALU.add),
                    inc=(S_V if c == NC - 1 else None))
            z_v = S_V.n
            r.op("gpsimd", lambda e, t=t: e.dma_start(
                out=z3d[:, :, tile_cols(t)], in_=ZT3), inc=ST[t % 2], inc_n=16,
                wait=[(S_V, z_v)])

            silu_done = {}
            for wi in range(3):
                par = ws_par[wi]
                for j in range(NC):
                    pcb = PB_A if j % 2 == 0 else PB_B
                    for c in range(NC):
                        w0 = []
                        if c == 0:
                            if j == 0:
                                w0 = [(SWS[par], SWS[par].n), (S_V, z_v)]
                            w0.append((S_SC, S_SC.n))
                        r.op("tensor", lambda e, c=c, j=j, par=par, pcb=pcb: e.matmul(
                            pcb[:, 0:TB],
                            WS[par][:, c * 1024 + j * P:c * 1024 + (j + 1) * P],
                            ZT3[:, c, :], start=(c == 0), stop=(c == NC - 1)),
                            inc=(S_T if c == NC - 1 else None), wait=w0)
                    mm_t = S_T.n
                    wpre = [(S_T, mm_t)]
                    if j == 0 and hasattr(r, "_ab_conv_v"):
                        wpre.append((S_V, r._ab_conv_v))
                    r.op("scalar", lambda e, j=j, pcb=pcb: e.activation(
                        PRE3[:, j, 4:4 + TB], pcb[:, 0:TB], AF.Copy), inc=S_SC,
                        wait=wpre)
                prev_ws_mm[par] = S_T.n
                if wi == 0:
                    emit_wstream(2)
                ev_s = S_SC.n
                r.op("vector", lambda e, wi=wi: e.tensor_copy(
                    PRE3[:, :, 0:4], HIST[wi][:].rearrange("p (c j) -> p c j", j=4)),
                    wait=[(S_SC, ev_s)])
                for c in range(NC):
                    for j in range(4):
                        if j == 0:
                            r.op("vector", lambda e, wi=wi, c=c: e.tensor_scalar(
                                CON3[:, c, :], PRE3[:, c, 1:1 + TB],
                                WCV[wi][:, c * 4:c * 4 + 1], None, ALU.mult))
                        else:
                            r.op("vector", lambda e, wi=wi, c=c, j=j:
                                 e.scalar_tensor_tensor(
                                     CON3[:, c, :], PRE3[:, c, 1 + j:1 + j + TB],
                                     WCV[wi][:, c * 4 + j:c * 4 + j + 1],
                                     CON3[:, c, :], ALU.mult, ALU.add))
                r.op("vector", lambda e, wi=wi: e.tensor_copy(
                    HIST[wi][:].rearrange("p (c j) -> p c j", j=4),
                    PRE3[:, :, TB:TB + 4]), inc=S_V)
                conv_v = S_V.n
                r._ab_conv_v = conv_v
                if wi == 2:
                    r.op("scalar", lambda e: e.activation(VIH3[:, :, 64:64 + TB],
                                                          CON3, AF.Silu),
                         inc=S_SC, wait=[(S_V, conv_v)])
                else:
                    OUTI = (QI3, KI3)[wi]
                    r.op("scalar", lambda e, OUTI=OUTI: e.activation(OUTI, CON3,
                                                                     AF.Silu),
                         inc=S_SC, wait=[(S_V, conv_v)])
                silu_done[wi] = S_SC.n
            r.op("gpsimd", lambda e, t=t: e.dma_start(
                out=vin3[:, :, tile_cols(t)], in_=VIH3[:, :, 64:64 + TB]),
                inc=ST[t % 2], inc_n=16, wait=[(S_SC, silu_done[2])])

            for c in range(NC):
                r.op("tensor", lambda e, c=c: e.matmul(
                    PB_R1[0:4, 0:TB], WBB[:, c * 4:(c + 1) * 4], ZT3[:, c, :],
                    start=(c == 0), stop=(c == NC - 1)),
                    inc=(S_T if c == NC - 1 else None),
                    wait=[(S_V, S_V.n)] if c == 0 else [])
            br_t = S_T.n
            r.op("scalar", lambda e: e.activation(ROWS4[0:4, 0:TB], PB_R1[0:4, 0:TB],
                                                  AF.Sigmoid), inc=S_SC,
                 wait=[(S_T, br_t)])

            for wi, (ti, roff) in enumerate(((QI3, 1), (KI3, 2))):
                r.op("scalar", lambda e, ti=ti: e.activation(SQ3, ti, AF.Square),
                     inc=S_SC, wait=[(S_SC, silu_done[wi]), (S_V, S_V.n),
                                     (S_T, S_T.n)])
                sqq_s = S_SC.n
                for c in range(NC):
                    r.op("tensor", lambda e, c=c: e.matmul(
                        PB_R3[0:4, 0:TB], HSEL[:, c * 4:(c + 1) * 4], SQ3[:, c, :],
                        start=(c == 0), stop=(c == NC - 1)),
                        inc=(S_T if c == NC - 1 else None),
                        wait=[(S_SC, sqq_s)] if c == 0 else [])
                r.op("vector", lambda e, roff=roff: e.tensor_scalar(
                    ROWS4[0:4, roff * TB:(roff + 1) * TB], PB_R3[0:4, 0:TB], 1e-6,
                    None, ALU.add), wait=[(S_T, S_T.n)], inc=S_V)
                r.op("scalar", lambda e, roff=roff: e.activation(
                    ROWS4[0:4, 4 * TB:5 * TB], ROWS4[0:4, roff * TB:(roff + 1) * TB],
                    AF.Sqrt), inc=S_SC, wait=[(S_V, S_V.n)])
                r.op("vector", lambda e, roff=roff: e.reciprocal(
                    ROWS4[0:4, roff * TB:(roff + 1) * TB],
                    ROWS4[0:4, 4 * TB:5 * TB]), wait=[(S_SC, S_SC.n)])
            r.op("vector", lambda e: e.tensor_tensor(ROWS4[0:4, 3 * TB:4 * TB],
                                                     ROWS4[0:4, 2 * TB:3 * TB],
                                                     ROWS4[0:4, 0:TB], ALU.mult))
            r.op("vector", lambda e: e.tensor_copy(RB4[0:4, 0:4 * TB],
                                                   ROWS4[0:4, 0:4 * TB]), inc=S_V)

            bj = ((1, QI3, QI3), (2, KI3, KI3), (0, KI3, KBN3), (0, None, VBT3))
            nb = 0
            for (ri, srcT, dstT) in bj:
                for h in range(H):
                    pcb = PB_A if nb % 2 == 0 else PB_B
                    nb += 1
                    r.op("tensor", lambda e, ri=ri, h=h, pcb=pcb: e.matmul(
                        pcb[:, 0:TB], E4S[:, h * P:(h + 1) * P],
                        RB4[0:4, ri * TB:(ri + 1) * TB], start=True, stop=True),
                        inc=S_T, wait=[(S_V, S_V.n)])
                    bc_t = S_T.n
                    for cc in range(2):
                        c = 2 * h + cc
                        if srcT is None:
                            r.op("vector", lambda e, c=c, pcb=pcb: e.tensor_tensor(
                                VBT3[:, c, :], VIH3[:, c, 64:64 + TB], pcb[:, 0:TB],
                                ALU.mult),
                                wait=[(S_T, bc_t)] + ([(S_SC, silu_done[2])]
                                                      if cc == 0 else []),
                                inc=(S_V if cc == 1 else None))
                        else:
                            r.op("vector", lambda e, c=c, srcT=srcT, dstT=dstT,
                                 pcb=pcb: e.tensor_tensor(
                                     dstT[:, c, :], srcT[:, c, :], pcb[:, 0:TB],
                                     ALU.mult), wait=[(S_T, bc_t)],
                                inc=(S_V if cc == 1 else None))

            for (srcT, dstM) in ((VBT3, VTM), (KI3, KTM)):
                for i in range(NCH):
                    for c in range(NC):
                        r.op("tensor", lambda e, srcT=srcT, c=c, i=i: e.transpose(
                            PB_T[0:64, (c % 4) * P:(c % 4 + 1) * P],
                            srcT[:, c, i * 64:(i + 1) * 64], IDB),
                            inc=(S_T if c % 4 == 3 else None),
                            wait=([(S_V, S_V.n)] if c % 4 == 0 else []))
                        if c % 4 == 3:
                            cg = c // 4
                            r.op("vector", lambda e, dstM=dstM, i=i, cg=cg:
                                 e.tensor_copy(
                                     dstM[0:64, i * 1024 + cg * 512:
                                          i * 1024 + (cg + 1) * 512],
                                     PB_T[0:64, 0:512]),
                                 wait=[(S_T, S_T.n)], inc=S_V)

            for i in range(NCH):
                for h in range(H):
                    for cc in range(2):
                        c = 2 * h + cc
                        r.op("tensor", lambda e, c=c, h=h, i=i, cc=cc: e.matmul(
                            PB_C[0:64, h * 64:(h + 1) * 64],
                            KBN3[:, c, i * 64:(i + 1) * 64],
                            KI3[:, c, i * 64:(i + 1) * 64],
                            start=(cc == 0), stop=(cc == 1)),
                            wait=([(S_V, S_V.n)] if (h == 0 and cc == 0)
                                  else []))
                for h in range(H):
                    for cc in range(2):
                        c = 2 * h + cc
                        r.op("tensor", lambda e, c=c, h=h, i=i, cc=cc: e.matmul(
                            PB_D[0:64, h * 64:(h + 1) * 64],
                            KI3[:, c, i * 64:(i + 1) * 64],
                            KBN3[:, c, i * 64:(i + 1) * 64],
                            start=(cc == 0), stop=(cc == 1)),
                            inc=(S_T if h == H - 1 and cc == 1 else None))
                bb_t = S_T.n
                r.op("vector", lambda e: e.tensor_tensor(P2M[:], PB_C[0:64, 0:256],
                                                         MSL4, ALU.mult),
                     wait=[(S_T, bb_t)])
                r.op("vector", lambda e: e.tensor_tensor(P2TM[:], PB_D[0:64, 0:256],
                                                         MSU4, ALU.mult))
                r.op("vector", lambda e: e.tensor_tensor(TTM[:], P2TM[:], ID4,
                                                         ALU.add), inc=S_V)
                for lvl in range(5):
                    for h in range(H):
                        r.op("tensor", lambda e, h=h: e.matmul(
                            PB_C[0:64, h * 64:(h + 1) * 64],
                            P2TM[0:64, h * 64:(h + 1) * 64],
                            P2M[0:64, h * 64:(h + 1) * 64],
                            start=True, stop=True),
                            wait=[(S_V, S_V.n)] if h == 0 else [],
                            inc=(S_T if (lvl == 4 and h == H - 1) else None))
                    if lvl < 4:
                        for h in range(H):
                            r.op("tensor", lambda e, h=h: e.matmul(
                                PB_D[0:64, h * 64:(h + 1) * 64],
                                P2M[0:64, h * 64:(h + 1) * 64],
                                P2TM[0:64, h * 64:(h + 1) * 64],
                                start=True, stop=True),
                                inc=(S_T if h == H - 1 else None))
                        sq_t = S_T.n
                        r.op("vector", lambda e: e.tensor_copy(P2M[:],
                                                               PB_C[0:64, 0:256]),
                             wait=[(S_T, sq_t)])
                        r.op("vector", lambda e: e.tensor_copy(
                            P2TM[:], PB_D[0:64, 0:256]), inc=S_V)
                    else:
                        sq_t = S_T.n
                        r.op("vector", lambda e: e.tensor_copy(P2M[:],
                                                               PB_C[0:64, 0:256]),
                             wait=[(S_T, sq_t)], inc=S_V)
                    p2_v = S_V.n
                    for h in range(H):
                        r.op("tensor", lambda e, h=h: e.matmul(
                            PB_C[0:64, h * 64:(h + 1) * 64],
                            C["ident_b"][0:64, 0:64], TTM[0:64, h * 64:(h + 1) * 64],
                            start=True, stop=False),
                            wait=[(S_V, p2_v)] if h == 0 else [])
                        r.op("tensor", lambda e, h=h: e.matmul(
                            PB_C[0:64, h * 64:(h + 1) * 64],
                            P2M[0:64, h * 64:(h + 1) * 64],
                            TTM[0:64, h * 64:(h + 1) * 64],
                            start=False, stop=True),
                            inc=(S_T if h == H - 1 else None))
                    up_t = S_T.n
                    r.op("vector", lambda e: e.tensor_copy(TTM[:], PB_C[0:64, 0:256]),
                         wait=[(S_T, up_t)], inc=S_V)

                for h in range(H):
                    for cc in range(2):
                        c = 2 * h + cc
                        r.op("tensor", lambda e, c=c, i=i, cc=cc: e.matmul(
                            PB_C[0:64, 256:512], KBN3[:, c, i * 64:(i + 1) * 64],
                            SSB[:, c * DV:(c + 1) * DV],
                            start=(cc == 0), stop=(cc == 1)),
                            inc=(S_T if cc == 1 else None),
                            wait=[(S_V, S_V.n), (S_SC, S_SC.n)] if cc == 0 else [])
                    y_t = S_T.n
                    r.op("vector", lambda e, i=i, h=h: e.tensor_tensor(
                        YB[:], VTM[0:64, i * 1024 + h * 256:i * 1024 + (h + 1) * 256],
                        PB_C[0:64, 256:512], ALU.subtract), inc=S_V,
                        wait=[(S_T, y_t)])
                    yv = S_V.n
                    r.op("tensor", lambda e, h=h: e.matmul(
                        PB_D[0:64, 0:256], TTM[0:64, h * 64:(h + 1) * 64], YB[:],
                        start=True, stop=True), inc=S_T, wait=[(S_V, yv)])
                    u2_t = S_T.n
                    r.op("vector", lambda e: e.tensor_copy(U2[:], PB_D[0:64, 0:256]),
                         inc=S_V, wait=[(S_T, u2_t)])
                    for cc in range(2):
                        c = 2 * h + cc
                        r.op("tensor", lambda e, c=c, i=i, h=h, cc=cc: e.matmul(
                            PB_C[0:64, 0:64], KI3[:, c, i * 64:(i + 1) * 64],
                            QI3[:, c, i * 64:(i + 1) * 64],
                            start=(cc == 0), stop=(cc == 1)),
                            inc=(S_T if cc == 1 else None))
                    at_t = S_T.n
                    r.op("vector", lambda e: e.tensor_tensor(
                        ATN[0:64, 0:64], PB_C[0:64, 0:64], MUI4[0:64, 0:64],
                        ALU.mult), inc=S_V, wait=[(S_T, at_t)])
                    atn_v = S_V.n
                    for cc in range(2):
                        c = 2 * h + cc
                        r.op("tensor", lambda e, c=c, i=i, cc=cc: e.matmul(
                            PB_C[0:64, 256:512], QI3[:, c, i * 64:(i + 1) * 64],
                            SSB[:, c * DV:(c + 1) * DV],
                            start=(cc == 0), stop=False),
                            wait=[(S_V, atn_v)] if cc == 0 else [])
                    r.op("tensor", lambda e: e.matmul(
                        PB_C[0:64, 256:512], ATN[0:64, 0:64], U2[:],
                        start=False, stop=True), inc=S_T)
                    o_t = S_T.n
                    r.op("vector", lambda e, i=i, h=h: e.tensor_copy(
                        DTM[0:64, i * 1024 + h * 256:i * 1024 + (h + 1) * 256],
                        PB_C[0:64, 256:512]), inc=S_V, wait=[(S_T, o_t)])
                    for cc in range(2):
                        c = 2 * h + cc
                        r.op("tensor", lambda e, c=c, cc=cc, i=i, h=h: e.matmul(
                            PB_D[0:P, cc * 256:(cc + 1) * 256],
                            KTM[0:64, i * 1024 + c * P:i * 1024 + (c + 1) * P],
                            U2[:], start=True, stop=True),
                            inc=(S_T if cc == 1 else None))
                    su_t = S_T.n
                    for cc in range(2):
                        c = 2 * h + cc
                        r.op("vector", lambda e, c=c, cc=cc: e.tensor_tensor(
                            SST[:, c * DV:(c + 1) * DV], SST[:, c * DV:(c + 1) * DV],
                            PB_D[0:P, cc * 256:(cc + 1) * 256], ALU.add),
                            wait=[(S_T, su_t)] if cc == 0 else [],
                            inc=(S_V if cc == 1 else None))
                    r.op("scalar", lambda e, h=h: e.activation(
                        SSB[:, 2 * h * DV:(2 * h + 2) * DV],
                        SST[:, 2 * h * DV:(2 * h + 2) * DV], AF.Copy), inc=S_SC,
                        wait=[(S_V, S_V.n)])
            for i in range(NCH):
                for cg in range(2):
                    for k4 in range(4):
                        c = cg * 4 + k4
                        r.op("tensor", lambda e, i=i, c=c, k4=k4: e.transpose(
                            PB_T[:, k4 * 64:(k4 + 1) * 64],
                            DTM[0:64, i * 1024 + c * 128:i * 1024 + (c + 1) * 128],
                            C["ident_b"][0:64, 0:64]),
                            inc=(S_T if k4 == 3 else None),
                            wait=([(S_V, S_V.n)] if k4 == 0 else []))
                    r.op("vector", lambda e, i=i, cg=cg: e.tensor_copy(
                        DEL3[:, cg * 4:(cg + 1) * 4, i * 64:(i + 1) * 64],
                        PB_T[0:P, 0:256].rearrange("p (c q) -> p c q", q=64)),
                        wait=[(S_T, S_T.n)], inc=S_V)
            r.op("gpsimd", lambda e, t=t: e.dma_start(
                out=dl3[:, :, tile_cols(t)], in_=DEL3), inc=ST[t % 2], inc_n=16,
                wait=[(S_V, S_V.n)])
        l1_T, l1_SC, l1_V = S_T.n, S_SC.n, S_V.n
        l1_ST = [ST[0].n, ST[1].n]

    with ExitStack() as l2s:
        sb2 = lambda name, shape, dt: l2s.enter_context(
            nc.sbuf_tensor(name + tag, shape, dt))
        VIHS = sb2("vihs", [P, NC * (STW + 64)], BF16)
        LSW = sb2("lsw", [P, NC * STW], BF16)
        LLW = sb2("llw", [P, NC * STW], BF16)
        DSTW = sb2("dstw", [P, NC * STW], BF16)
        ZSB = sb2("zs0", [P, NC * SWD], BF16)
        OBS = sb2("obs", [P, NC * SWD], BF16)
        GBS = sb2("gbs", [P, NC * SWD], BF16)
        SQW = sb2("sqw", [P, NC * SWD], BF16)
        ABW = sb2("abw", [P, NC * SWD], BF16)
        WG1Z = sb2("wg1z", [P, NC * 1024], BF16)
        WOO = sb2("woo", [P, NC * 1024], BF16)
        WOG1Z = sb2("wog1z", [P, NC * 512], BF16)
        RB16W = sb2("rb16w", [4, 16 * SWD], BF16)
        STRW = sb2("strw", [4, 4 * SWD], F32)
        R4W = sb2("r4w", [4, 3 * SWD], F32)
        ROWS64W = sb2("rows64w", [64, SWD], BF16)
        OGR32W = sb2("ogr32w", [32, SWD], BF16)
        LGTW = sb2("lgtw", [P, 16], F32)
        PTMW = sb2("ptmw", [P, 16], F32)
        TMSW = sb2("tmsw", [P, 4], F32)
        PFMBW = sb2("pfmbw", [4, SWD], BF16)
        WROWBW = sb2("wrowbw", [4, 4 * SWD], BF16)
        RMRW = sb2("rmrw", [4, SWD], F32)
        RR2BW = sb2("rr2bw", [4, SWD], BF16)
        OGROWBW = sb2("ogrowbw", [1, SWD], BF16)

        VIS3 = VIHS[:].rearrange("p (c q) -> p c q", c=NC)
        LSW3 = LSW[:].rearrange("p (c q) -> p c q", c=NC)
        LLW3 = LLW[:].rearrange("p (c q) -> p c q", c=NC)
        DST3 = DSTW[:].rearrange("p (c q) -> p c q", c=NC)
        ZS3 = ZSB[:].rearrange("p (c q) -> p c q", c=NC)
        OBS3 = OBS[:].rearrange("p (c q) -> p c q", c=NC)
        GBS3 = GBS[:].rearrange("p (c q) -> p c q", c=NC)
        SQW3 = SQW[:].rearrange("p (c q) -> p c q", c=NC)
        ABW3 = ABW[:].rearrange("p (c q) -> p c q", c=NC)
        XR0_3 = SQW[:].bitcast(F32).rearrange("p (c q) -> p c q", c=NC)
        XR1_3 = ABW[:].bitcast(F32).rearrange("p (c q) -> p c q", c=NC)

        quiA = [(S_T, l1_T), (S_SC, l1_SC), (S_V, l1_V),
                (ST[0], l1_ST[0]), (ST[1], l1_ST[1])]
        r.op("sync", lambda e: e.dma_start(
            out=WG1Z[:].rearrange("p (c d) -> p c d", c=NC),
            in_=scr[3].rearrange("(c p) d -> p c d", p=P)), inc=S_LW, inc_n=16,
            wait=quiA)
        r.op("sync", lambda e: e.dma_start(
            out=WOO[:].rearrange("p (c d) -> p c d", c=NC),
            in_=scr[4].rearrange("(c p) d -> p c d", p=P)), inc=S_LW, inc_n=16,
            wait=[(S_LW, S_LW.n)])
        r.op("sync", lambda e: e.dma_start(
            out=WOG1Z[:].rearrange("p (c d) -> p c d", c=NC),
            in_=scr[5].rearrange("(c p) d -> p c d", p=P)), inc=S_LW, inc_n=16,
            wait=[(S_LW, S_LW.n)])
        res_lw = S_LW.n
        r.wait("tensor", quiA)
        r.wait("scalar", quiA)

        first_v = True
        for st in range(NST):
            base = st * STW
            S_LD = LD[st % 2]
            S_DD = RD[st % 2]
            if st == 0:
                r.op("vector", lambda e: e.memset(VIS3[:, :, 0:64], 0.0),
                     wait=quiA, inc=S_V)
                r.op("sync", lambda e: e.dma_start(
                    out=VIS3[:, :, 64:64 + STW], in_=vin3[:, :, 0:STW]),
                    inc=S_LD, inc_n=16, wait=[(S_V, S_V.n)])
            else:
                r.op("sync", lambda e, base=base: e.dma_start(
                    out=VIS3[:, :, 0:64 + STW],
                    in_=vin3[:, :, base - 64:base + STW]),
                    inc=S_LD, inc_n=16,
                    wait=[(S_V, S_V.n), (S_SC, S_SC.n)])
            vih_ld = S_LD.n
            r.op("sync", lambda e, base=base: e.dma_start(
                out=DST3, in_=dl3[:, :, base:base + STW]), inc=S_DD, inc_n=16,
                wait=[(S_V, S_V.n)])
            dst_ld = S_DD.n

            firfirst = True
            for (dst3, wten, K) in ((LSW3, FIRS, 5), (LLW3, FIRL, 64)):
                bse = 64 - K + 1
                for c in range(NC):
                    w0 = ([(S_LD, vih_ld), (S_SC, S_SC.n)] if firfirst else [])
                    firfirst = False
                    r.op("vector", lambda e, dst3=dst3, c=c, wten=wten, K=K,
                         bse=bse: e.tensor_scalar(
                             dst3[:, c, :], VIS3[:, c, bse:bse + STW],
                             wten[:, c * K:c * K + 1], None, ALU.mult), wait=w0)
                    for j in range(1, K):
                        r.op("vector", lambda e, dst3=dst3, c=c, j=j, wten=wten,
                             K=K, bse=bse: e.scalar_tensor_tensor(
                                 dst3[:, c, :], VIS3[:, c, bse + j:bse + j + STW],
                                 wten[:, c * K + j:c * K + j + 1], dst3[:, c, :],
                                 ALU.mult, ALU.add))
            r.op("vector", lambda e: e.engine_nop(), inc=S_V)
            fir_v = S_V.n

            L2STOP = int(os.environ.get('AB_L2STOP', 99))
            for s in range(NSP):
                sidx = st * NSP + s
                par = sidx % 2
                soff = s * SWD
                zs3 = ZS3
                zwl = [(S_T, r._p_zs)] if hasattr(r, "_p_zs") else []
                r.op("sync", lambda e, base=base, soff=soff, zs3=zs3: e.dma_start(
                    out=zs3, in_=z3d[:, :, base + soff:base + soff + SWD]),
                    inc=SWS[par], inc_n=16, wait=zwl)
                zs_ld = SWS[par].n

                srcs = (lambda c, soff=soff: LSW3[:, c, soff:soff + SWD],
                        lambda c, soff=soff: LLW3[:, c, soff:soff + SWD],
                        lambda c, soff=soff: DST3[:, c, soff:soff + SWD],
                        lambda c, soff=soff: VIS3[:, c, 64 + soff:64 + soff + SWD])
                for tau in range(4):
                    gsrc = srcs[tau]
                    w0s = [(S_V, fir_v if tau == 0 else S_V.n), (S_T, S_T.n)]
                    if tau == 0 and sidx > 0:
                        w0s.append((ST[1 - par], ST[1 - par].n))
                    if tau == 2:
                        w0s.append((S_DD, dst_ld))
                    src_full = (lambda e, soff=soff: e.activation(
                        SQW3, VIS3[:, :, 64 + soff:64 + soff + SWD], AF.Square)) \
                        if tau == 3 else None
                    full3 = (LSW3, LLW3, DST3, None)[tau]
                    if tau == 3:
                        r.op("scalar", lambda e, soff=soff: e.activation(
                            SQW3, VIS3[:, :, 64 + soff:64 + soff + SWD],
                            AF.Square), wait=w0s)
                        r.op("scalar", lambda e, soff=soff: e.activation(
                            ABW3, VIS3[:, :, 64 + soff:64 + soff + SWD],
                            AF.Abs), inc=S_SC)
                    else:
                        r.op("scalar", lambda e, full3=full3, soff=soff:
                             e.activation(SQW3, full3[:, :, soff:soff + SWD],
                                          AF.Square), wait=w0s)
                        r.op("scalar", lambda e, full3=full3, soff=soff:
                             e.activation(ABW3, full3[:, :, soff:soff + SWD],
                                          AF.Abs), inc=S_SC)
                    sub_s = S_SC.n
                    for c in range(NC):
                        r.op("tensor", lambda e, c=c, gsrc=gsrc: e.matmul(
                            PB_R1[0:4, 0:SWD], HSEL[:, c * 4:(c + 1) * 4], gsrc(c),
                            start=(c == 0), stop=(c == NC - 1)),
                            wait=[(S_SC, sub_s), (S_V, S_V.n)] if c == 0 else [])
                    for c in range(NC):
                        r.op("tensor", lambda e, c=c: e.matmul(
                            PB_R2[0:4, 0:SWD], HSEL[:, c * 4:(c + 1) * 4],
                            SQW3[:, c, :], start=(c == 0), stop=(c == NC - 1)))
                    for c in range(NC):
                        r.op("tensor", lambda e, c=c: e.matmul(
                            PB_R3[0:4, 0:SWD], HSEL[:, c * 4:(c + 1) * 4],
                            ABW3[:, c, :], start=(c == 0), stop=(c == NC - 1)),
                            inc=(S_T if c == NC - 1 else None))
                    stt = S_T.n
                    reg = 4 * tau * SWD
                    wreg = [(S_T, stt)]
                    if sidx > 0 and tau == 0:
                        wreg.append(r._p_rows_rd)
                    r.op("vector", lambda e: e.tensor_scalar(
                        STRW[0:4, 0:SWD], PB_R1[0:4, 0:SWD], 1.0 / DV, None,
                        ALU.mult), wait=wreg)
                    r.op("vector", lambda e: e.tensor_tensor(
                        STRW[0:4, SWD:2 * SWD], STRW[0:4, 0:SWD],
                        STRW[0:4, 0:SWD], ALU.mult))
                    r.op("vector", lambda e, reg=reg: e.tensor_copy(
                        RB16W[0:4, reg:reg + SWD], STRW[0:4, 0:SWD]))
                    r.op("vector", lambda e: e.tensor_scalar(
                        STRW[0:4, 2 * SWD:3 * SWD], PB_R2[0:4, 0:SWD], 1.0 / DV,
                        None, ALU.mult))
                    r.op("vector", lambda e, reg=reg: e.tensor_tensor(
                        RB16W[0:4, reg + SWD:reg + 2 * SWD],
                        STRW[0:4, 2 * SWD:3 * SWD], STRW[0:4, SWD:2 * SWD],
                        ALU.subtract))
                    r.op("vector", lambda e, reg=reg: e.tensor_scalar(
                        RB16W[0:4, reg + 2 * SWD:reg + 3 * SWD], PB_R3[0:4, 0:SWD],
                        1.0 / DV, None, ALU.mult), inc=S_V)
                    r.op("scalar", lambda e, reg=reg: e.activation(
                        RB16W[0:4, reg + 3 * SWD:reg + 4 * SWD], PB_R2[0:4, 0:SWD],
                        AF.Sqrt), inc=S_SC, wait=[(S_V, S_V.n)])
                for h4 in range(4):
                    wrd = [(S_SC, S_SC.n)]
                    if h4 == 0:
                        wrd.append((S_V, S_V.n))
                        if sidx > 0:
                            wrd.append((S_T, r._p_gate_t))
                    r.op("sync", lambda e, h4=h4: e.dma_start(
                        out=rscr["rows"].rearrange("(f hh) q -> hh f q", hh=4)[h4],
                        in_=RB16W[h4:h4 + 1, 0:16 * SWD]
                        .rearrange("one (f q) -> one f q", q=SWD)),
                        inc=S_DD, inc_n=16, wait=wrd)
                r.op("sync", lambda e: e.dma_start(out=ROWS64W[:, 0:SWD],
                                                   in_=rscr["rows"]),
                     inc=S_DD, inc_n=16, wait=[(S_DD, S_DD.n)])
                rows_rd = S_DD.n
                r._p_rows_rd = (S_DD, rows_rd)

                for c in range(NC):
                    r.op("tensor", lambda e, c=c, zs3=zs3: e.matmul(
                        PB_R3[0:4, 0:SWD], WRESB[:, c * 4:(c + 1) * 4],
                        zs3[:, c, :], start=(c == 0), stop=(c == NC - 1)),
                        inc=(S_T if c == NC - 1 else None),
                        wait=([(SWS[par], zs_ld), (S_V, S_V.n)] if c == 0 else []))
                res_t = S_T.n
                r.op("scalar", lambda e: e.activation(
                    R4W[0:4, SWD:2 * SWD], PB_R3[0:4, 0:SWD], AF.Sigmoid,
                    bias=BCOL[0:4, 1:2]), inc=S_SC, wait=[(S_T, res_t)])
                r.op("vector", lambda e: e.tensor_scalar(
                    R4W[0:4, 2 * SWD:3 * SWD], R4W[0:4, SWD:2 * SWD],
                    BCOL[0:4, 3:4], None, ALU.mult), wait=[(S_SC, S_SC.n)],
                    inc=S_V)

                for j in range(NC):
                    pcb = PB_A if j % 2 == 0 else PB_B
                    for c in range(NC):
                        w0 = []
                        if c == 0:
                            w0 = [(S_SC, S_SC.n)]
                            if j == 0:
                                w0 += [(S_DD, rows_rd), (S_LW, res_lw)]
                        r.op("tensor", lambda e, c=c, j=j, pcb=pcb, zs3=zs3:
                             e.matmul(
                                 pcb[:, 0:SWD],
                                 WG1Z[:, c * 1024 + j * P:c * 1024 + (j + 1) * P],
                                 zs3[:, c, :], start=(c == 0), stop=False),
                             wait=w0)
                    r.op("tensor", lambda e, j=j, pcb=pcb: e.matmul(
                        pcb[:, 0:SWD], W4[:, j * P:(j + 1) * P],
                        ROWS64W[0:64, 0:SWD], start=False, stop=True), inc=S_T)
                    g_t = S_T.n
                    r.op("scalar", lambda e, j=j, pcb=pcb: e.activation(
                        GBS3[:, j, :], pcb[:, 0:SWD], AF.Gelu,
                        bias=BG1[:, j:j + 1]), inc=S_SC, wait=[(S_T, g_t)])
                r._p_gate_t = S_T.n
                g1_s = S_SC.n
                for c in range(NC):
                    r.op("tensor", lambda e, c=c: e.matmul(
                        PB_R1[0:4, 0:SWD], WG2B[:, c * 4:(c + 1) * 4],
                        GBS3[:, c, :], start=(c == 0), stop=(c == NC - 1)),
                        inc=(S_T if c == NC - 1 else None),
                        wait=[(S_SC, g1_s), (S_V, S_V.n)] if c == 0 else [])
                lg_t = S_T.n
                r.op("scalar", lambda e: e.activation(
                    R4W[0:4, 0:SWD], PB_R1[0:4, 0:SWD], AF.Identity,
                    bias=RSP4[0:4, 1:2], scale=RSP4[0:4, 0:1]), inc=S_SC,
                    wait=[(S_T, lg_t)])
                r.wait("tensor", [(S_SC, S_SC.n)])
                for ts in range(4):
                    r.op("tensor", lambda e, ts=ts: e.matmul(
                        PB_D[0:P, ts * 4:(ts + 1) * 4],
                        R4W[0:4, ts * P:(ts + 1) * P],
                        C["ident_f"][0:4, 0:4], start=True, stop=True,
                        is_transpose=True),
                        inc=(S_T if ts == 3 else None))
                ltr_t = S_T.n
                r.op("vector", lambda e: e.tensor_copy(LGTW[:, 0:16],
                                                       PB_D[0:P, 0:16]),
                     wait=[(S_T, ltr_t)])
                for ts in range(4):
                    r.op("vector", lambda e, ts=ts: e.tensor_reduce(
                        TMSW[:, 0:1], LGTW[:, 4 * ts:4 * ts + 4], AX.X, ALU.max))
                    r.op("vector", lambda e: e.tensor_scalar(
                        TMSW[:, 1:2], TMSW[:, 0:1], -1.0, None, ALU.mult))
                    r.op("vector", lambda e, ts=ts: e.tensor_scalar(
                        PTMW[:, 4 * ts:4 * ts + 4], LGTW[:, 4 * ts:4 * ts + 4],
                        TMSW[:, 1:2], None, ALU.add),
                        inc=(S_V if ts == 3 else None))
                r.op("scalar", lambda e: e.activation(PTMW[:, 0:16], PTMW[:, 0:16],
                                                      AF.Exp),
                     inc=S_SC, wait=[(S_V, S_V.n)])
                r.wait("vector", [(S_SC, S_SC.n)])
                for ts in range(4):
                    r.op("vector", lambda e, ts=ts: e.tensor_reduce(
                        TMSW[:, 0:1], PTMW[:, 4 * ts:4 * ts + 4], AX.X, ALU.add))
                    r.op("vector", lambda e: e.reciprocal(TMSW[:, 1:2],
                                                          TMSW[:, 0:1]))
                    r.op("vector", lambda e, ts=ts: e.tensor_scalar(
                        PTMW[:, 4 * ts:4 * ts + 4], PTMW[:, 4 * ts:4 * ts + 4],
                        TMSW[:, 1:2], None, ALU.mult))
                    r.op("vector", lambda e, ts=ts: e.tensor_scalar(
                        PTMW[:, 4 * ts:4 * ts + 4], PTMW[:, 4 * ts:4 * ts + 4],
                        0.02, None, ALU.max))
                    r.op("vector", lambda e, ts=ts: e.tensor_reduce(
                        TMSW[:, 0:1], PTMW[:, 4 * ts:4 * ts + 4], AX.X, ALU.add))
                    r.op("vector", lambda e: e.reciprocal(TMSW[:, 1:2],
                                                          TMSW[:, 0:1]))
                    r.op("vector", lambda e, ts=ts: e.tensor_scalar(
                        PTMW[:, 4 * ts:4 * ts + 4], PTMW[:, 4 * ts:4 * ts + 4],
                        TMSW[:, 1:2], None, ALU.mult),
                        inc=(S_V if ts == 3 else None))
                ptm_v = S_V.n
                for ts in range(4):
                    r.op("tensor", lambda e, ts=ts: e.matmul(
                        PB_R1[0:4, ts * P:(ts + 1) * P], PTMW[:, 4 * ts:4 * ts + 4],
                        C["ident_f"], start=True, stop=True, is_transpose=True),
                        inc=(S_T if ts == 3 else None),
                        wait=[(S_V, ptm_v)] if ts == 0 else [])
                pf_t = S_T.n
                r.op("vector", lambda e: e.tensor_copy(PFMBW[0:4, 0:SWD],
                                                       PB_R1[0:4, 0:SWD]),
                     wait=[(S_T, pf_t)], inc=S_V)
                pfm_v = S_V.n
                for b2 in range(4):
                    r.op("tensor", lambda e, b2=b2: e.matmul(
                        PB_R2[0:4, 0:SWD], E4S[0:4, b2 * P:b2 * P + 4],
                        PFMBW[0:4, 0:SWD], start=True, stop=True),
                        inc=S_T, wait=[(S_V, pfm_v)] if b2 == 0 else
                        [(S_V, S_V.n)])
                    wr_t = S_T.n
                    if b2 == 0:
                        r.op("vector", lambda e: e.tensor_tensor(
                            WROWBW[0:4, 0:SWD], PB_R2[0:4, 0:SWD],
                            R4W[0:4, 2 * SWD:3 * SWD], ALU.add),
                            wait=[(S_T, wr_t)])
                    else:
                        r.op("vector", lambda e, b2=b2: e.tensor_copy(
                            WROWBW[0:4, b2 * SWD:(b2 + 1) * SWD],
                            PB_R2[0:4, 0:SWD]), wait=[(S_T, wr_t)])
                r.op("vector", lambda e: e.engine_nop(), inc=S_V)
                nbc = 0
                for h in range(H):
                    for b2 in range(4):
                        pcb = PB_A if nbc % 2 == 0 else PB_B
                        nbc += 1
                        r.op("tensor", lambda e, h=h, b2=b2, pcb=pcb: e.matmul(
                            pcb[:, 0:SWD], E4S[:, h * P:(h + 1) * P],
                            WROWBW[0:4, b2 * SWD:(b2 + 1) * SWD], start=True,
                            stop=True), inc=S_T, wait=[(S_V, S_V.n)])
                        bc_t = S_T.n
                        for cc in range(2):
                            c = 2 * h + cc
                            gs = srcs[b2](c)
                            if b2 == 0:
                                r.op("vector", lambda e, c=c, gs=gs, pcb=pcb:
                                     e.tensor_tensor(OBS3[:, c, :], gs,
                                                     pcb[:, 0:SWD], ALU.mult),
                                     wait=[(S_T, bc_t)] if cc == 0 else [],
                                     inc=(S_V if cc == 1 else None))
                            else:
                                r.op("vector", lambda e, c=c, gs=gs, pcb=pcb:
                                     e.tensor_tensor(GBS3[:, c, :], gs,
                                                     pcb[:, 0:SWD], ALU.mult),
                                     wait=[(S_T, bc_t)] if cc == 0 else [])
                                r.op("vector", lambda e, c=c: e.tensor_tensor(
                                    OBS3[:, c, :], OBS3[:, c, :], GBS3[:, c, :],
                                    ALU.add), inc=(S_V if cc == 1 else None))
                comb_v = S_V.n
                r.op("scalar", lambda e: e.activation(SQW3, OBS3, AF.Square),
                     wait=[(S_V, comb_v), (S_SC, S_SC.n), (S_T, S_T.n)])
                r.op("scalar", lambda e: e.activation(ABW3, OBS3, AF.Abs),
                     inc=S_SC)
                osub_s = S_SC.n
                for c in range(NC):
                    r.op("tensor", lambda e, c=c: e.matmul(
                        PB_R1[0:4, 0:SWD], HSEL[:, c * 4:(c + 1) * 4],
                        OBS3[:, c, :], start=(c == 0), stop=(c == NC - 1)),
                        wait=[(S_SC, osub_s), (S_V, S_V.n)] if c == 0 else [])
                for c in range(NC):
                    r.op("tensor", lambda e, c=c: e.matmul(
                        PB_R2[0:4, 0:SWD], HSEL[:, c * 4:(c + 1) * 4],
                        SQW3[:, c, :], start=(c == 0), stop=(c == NC - 1)))
                for c in range(NC):
                    r.op("tensor", lambda e, c=c: e.matmul(
                        PB_R3[0:4, 0:SWD], HSEL[:, c * 4:(c + 1) * 4],
                        ABW3[:, c, :], start=(c == 0), stop=(c == NC - 1)),
                        inc=(S_T if c == NC - 1 else None))
                ost_t = S_T.n
                wreg = [(S_T, ost_t), (S_DD, rows_rd)]
                r.op("vector", lambda e: e.tensor_scalar(
                    STRW[0:4, 0:SWD], PB_R1[0:4, 0:SWD], 1.0 / DV, None,
                    ALU.mult), wait=wreg)
                r.op("vector", lambda e: e.tensor_tensor(
                    STRW[0:4, SWD:2 * SWD], STRW[0:4, 0:SWD], STRW[0:4, 0:SWD],
                    ALU.mult))
                r.op("vector", lambda e: e.tensor_copy(
                    RB16W[0:4, 0:SWD], STRW[0:4, 0:SWD]))
                r.op("vector", lambda e: e.tensor_scalar(
                    STRW[0:4, 3 * SWD:4 * SWD], PB_R2[0:4, 0:SWD], 1.0 / DV,
                    None, ALU.mult))
                r.op("vector", lambda e: e.tensor_tensor(
                    RB16W[0:4, SWD:2 * SWD], STRW[0:4, 3 * SWD:4 * SWD],
                    STRW[0:4, SWD:2 * SWD], ALU.subtract))
                r.op("vector", lambda e: e.tensor_scalar(
                    RB16W[0:4, 2 * SWD:3 * SWD], PB_R3[0:4, 0:SWD], 1.0 / DV,
                    None, ALU.mult), inc=S_V)
                r.op("scalar", lambda e: e.activation(
                    RB16W[0:4, 3 * SWD:4 * SWD], PB_R2[0:4, 0:SWD], AF.Sqrt),
                    inc=S_SC, wait=[(S_V, S_V.n)])
                for h4 in range(4):
                    wrd = [(S_SC, S_SC.n)]
                    if h4 == 0:
                        wrd.append((S_V, S_V.n))
                        if sidx > 0:
                            wrd.append((S_T, r._p_og_t))
                    r.op("sync", lambda e, h4=h4: e.dma_start(
                        out=rscr["ogr"].rearrange("(f hh) q -> hh f q", hh=4)
                        [h4][0:4, :],
                        in_=RB16W[h4:h4 + 1, 0:4 * SWD]
                        .rearrange("one (f q) -> one f q", q=SWD)),
                        inc=S_DD, inc_n=16, wait=wrd)
                    r.op("sync", lambda e, h4=h4: e.dma_start(
                        out=rscr["ogr"].rearrange("(f hh) q -> hh f q", hh=4)
                        [h4][4:8, :],
                        in_=RB16W[h4:h4 + 1, 12 * SWD:16 * SWD]
                        .rearrange("one (f q) -> one f q", q=SWD)),
                        inc=S_DD, inc_n=16)
                r.op("sync", lambda e: e.dma_start(out=OGR32W[:, 0:SWD],
                                                   in_=rscr["ogr"]),
                     inc=S_DD, inc_n=16, wait=[(S_DD, S_DD.n)])
                ogr_rd = S_DD.n
                for j in range(4):
                    pcb = PB_A if j % 2 == 0 else PB_B
                    for c in range(NC):
                        w0 = []
                        if c == 0:
                            w0 = [(S_SC, S_SC.n)]
                            if j == 0:
                                w0.append((S_DD, ogr_rd))
                        r.op("tensor", lambda e, c=c, j=j, pcb=pcb, zs3=zs3:
                             e.matmul(
                                 pcb[:, 0:SWD],
                                 WOG1Z[:, c * 512 + j * P:c * 512 + (j + 1) * P],
                                 zs3[:, c, :], start=(c == 0), stop=False),
                             wait=w0)
                    r.op("tensor", lambda e, j=j, pcb=pcb: e.matmul(
                        pcb[:, 0:SWD], W4OG[:, j * P:(j + 1) * P],
                        OGR32W[0:32, 0:SWD], start=False, stop=True), inc=S_T)
                    og_t = S_T.n
                    r.op("scalar", lambda e, j=j, pcb=pcb: e.activation(
                        GBS3[:, j, :], pcb[:, 0:SWD], AF.Gelu,
                        bias=BOG1[:, j:j + 1]), inc=S_SC, wait=[(S_T, og_t)])
                r._p_og_t = S_T.n
                og1_s = S_SC.n
                for c in range(4):
                    r.op("tensor", lambda e, c=c: e.matmul(
                        PB_R1[0:1, 0:SWD], WOG2B[:, c:c + 1], GBS3[:, c, :],
                        start=(c == 0), stop=(c == 3)),
                        inc=(S_T if c == 3 else None),
                        wait=[(S_SC, og1_s), (S_V, S_V.n)] if c == 0 else [])
                ogr_t = S_T.n
                r.op("scalar", lambda e: e.activation(
                    OGROWBW[0:1, 0:SWD], PB_R1[0:1, 0:SWD], AF.Sigmoid,
                    bias=LT[0:1, 4:5]), inc=S_SC, wait=[(S_T, ogr_t)])
                r.op("tensor", lambda e: e.matmul(
                    PB_R2[0:4, 0:SWD], C["ones1_b"][0:1, 0:4], OGROWBW[0:1, 0:SWD],
                    start=True, stop=True), inc=S_T, wait=[(S_SC, S_SC.n)])
                ogb_t = S_T.n
                r.op("vector", lambda e: e.tensor_copy(RMRW[0:4, 0:SWD],
                                                       PB_R2[0:4, 0:SWD]),
                     wait=[(S_T, ogb_t)])
                r.op("vector", lambda e: e.tensor_tensor(
                    STRW[0:4, SWD:2 * SWD], RMRW[0:4, 0:SWD], RMRW[0:4, 0:SWD],
                    ALU.mult))
                r.op("vector", lambda e: e.tensor_tensor(
                    STRW[0:4, SWD:2 * SWD], STRW[0:4, SWD:2 * SWD],
                    STRW[0:4, 3 * SWD:4 * SWD], ALU.mult))
                r.op("vector", lambda e: e.tensor_scalar(
                    STRW[0:4, SWD:2 * SWD], STRW[0:4, SWD:2 * SWD], 1e-5, None,
                    ALU.add), inc=S_V)
                r.op("scalar", lambda e: e.activation(
                    STRW[0:4, 2 * SWD:3 * SWD], STRW[0:4, SWD:2 * SWD], AF.Sqrt),
                    inc=S_SC, wait=[(S_V, S_V.n)])
                r.op("vector", lambda e: e.reciprocal(STRW[0:4, SWD:2 * SWD],
                                                      STRW[0:4, 2 * SWD:3 * SWD]),
                     wait=[(S_SC, S_SC.n)])
                r.op("vector", lambda e: e.tensor_tensor(
                    RMRW[0:4, 0:SWD], RMRW[0:4, 0:SWD], STRW[0:4, SWD:2 * SWD],
                    ALU.mult))
                r.op("vector", lambda e: e.tensor_copy(RR2BW[0:4, 0:SWD],
                                                       RMRW[0:4, 0:SWD]),
                     inc=S_V)
                nbc = 0
                for h in range(H):
                    pcb = PB_A if nbc % 2 == 0 else PB_B
                    nbc += 1
                    r.op("tensor", lambda e, h=h, pcb=pcb: e.matmul(
                        pcb[:, 0:SWD], E4S[:, h * P:(h + 1) * P],
                        RR2BW[0:4, 0:SWD], start=True, stop=True), inc=S_T,
                        wait=[(S_V, S_V.n)])
                    nr_t = S_T.n
                    for cc in range(2):
                        c = 2 * h + cc
                        r.op("vector", lambda e, c=c, cc=cc, pcb=pcb:
                             e.scalar_tensor_tensor(
                                 OBS3[:, c, :], OBS3[:, c, :], NW[:, cc:cc + 1],
                                 pcb[:, 0:SWD], ALU.mult, ALU.mult),
                             wait=[(S_T, nr_t)] if cc == 0 else [],
                             inc=(S_V if cc == 1 else None))
                on_v = S_V.n
                for j in range(NC):
                    pcb = PB_A if j % 2 == 0 else PB_B
                    for c in range(NC):
                        w0 = []
                        if c == 0:
                            w0 = [(S_SC, S_SC.n)]
                            if j == 0:
                                w0.append((S_V, on_v))
                        r.op("tensor", lambda e, c=c, j=j, pcb=pcb: e.matmul(
                            pcb[:, 0:SWD],
                            WOO[:, c * 1024 + j * P:c * 1024 + (j + 1) * P],
                            OBS3[:, c, :], start=(c == 0), stop=(c == NC - 1)),
                            inc=(S_T if c == NC - 1 else None), wait=w0)
                    wo_t = S_T.n
                    r.op("scalar", lambda e, j=j, pcb=pcb: e.activation(
                        GBS3[:, j, :], pcb[:, 0:SWD], AF.Copy), inc=S_SC,
                        wait=[(S_T, wo_t)])
                r._p_zs = S_T.n
                wo_s = S_SC.n
                for hh in range(2):
                    hw = [(S_SC, wo_s), (S_T, S_T.n)]
                    if hh == 1:
                        hw.append((S_V, S_V.n))
                    r.op("sync", lambda e, base=base, soff=soff, hh=hh:
                         e.dma_start(
                             out=XR0_3,
                             in_=x3(x_in)[:, :, base + soff + hh * 256:
                                          base + soff + hh * 256 + 256]),
                         inc=S_DD, inc_n=16, wait=hw)
                    xr_ld = S_DD.n
                    stw = [(S_DD, xr_ld)]
                    if hh == 1:
                        stw.append((ST[par], ST[par].n))
                    r.op("vector", lambda e, hh=hh: e.tensor_tensor(
                        XR1_3, XR0_3,
                        GBS3[:, :, hh * 256:(hh + 1) * 256], ALU.add),
                        wait=stw, inc=S_V)
                    out_v = S_V.n
                    r.op("gpsimd", lambda e, base=base, soff=soff, hh=hh:
                         e.dma_start(
                             out=x3(x_out)[:, :, base + soff + hh * 256:
                                           base + soff + hh * 256 + 256],
                             in_=XR1_3),
                         inc=ST[par], inc_n=16, wait=[(S_V, out_v)])

    r.wait("gpsimd", [(ST[0], ST[0].n), (ST[1], ST[1].n)])



def build_program(phases, debug=False):
    global LAST_REC
    nc = bass.Bass("TRN2", target_bir_lowering=False, debug=False)
    es = ExitStack()
    r = Rec(nc, es)
    LAST_REC = r

    dram = {}

    def din(name, shape, dt=F32):
        dram[name] = nc.dram_tensor(name, shape, dt, kind="ExternalInput").ap()
        return dram[name]

    xin = din("xT", [D, T])
    xout = nc.dram_tensor("outT", [D, T], F32, kind="ExternalOutput").ap()
    xA = nc.dram_tensor("xA", [D, T], F32, kind="Internal").ap()
    xB = nc.dram_tensor("xB", [D, T], F32, kind="Internal").ap()
    y_scr = nc.dram_tensor("y_scr", [D, T], BF16, kind="Internal").ap()

    cdr = {}
    cspec = {
        "ones128_b": ([P, 1], BF16), "ones128_f": ([P, 1], F32),
        "ones1_b": ([1, P], BF16), "ones1_f": ([1, P], F32),
        "ones1_f1": ([1, 1], F32), "ident_f": ([P, P], F32),
        "iota15": ([1, 15], F32), "bar": ([1, 4], F32), "barw": ([1, 1], BF16),
        "ident_b": ([P, P], BF16), "ones2h": ([P, 2], BF16),
        "e4s": ([4, 512], BF16), "headsel": ([P, 32], BF16),
        "msl4": ([64, 256], F32), "msu4": ([64, 256], F32),
        "mui4": ([64, 256], BF16), "id4": ([64, 256], BF16),
    }
    for k, (shape, dt) in cspec.items():
        cdr[k] = din("c_" + k, shape, dt)

    wffn = {"w_router": din("w_routerL", [P, NC]),
            "w_block": din("L0_ffn_w_block", [D, D])}
    need_a = "attn_a" in phases
    if need_a:
        watt = {
            "a_Wq": din("a_Wq", [D, D]), "a_Wk": din("a_Wk", [D, D]),
            "a_Wv": din("a_Wv", [D, D]), "a_Wo": din("a_Wo", [D, D]),
            "a_bq": din("a_bq", [P, NC]), "a_bk": din("a_bk", [P, NC]),
            "a_bv": din("a_bv", [P, NC]), "a_bo": din("a_bo", [P, NC]),
            "a_pbT": din("a_pbT", [16, 512, 512]),
            "a_cntw": din("a_cntw", [P, 64]),
        }
        ctx_scr = nc.dram_tensor("ctx_scr", [D, T], BF16,
                                 kind="ExternalOutput" if debug else "Internal").ap()
    if "attn_b" in phases:
        wab = {}
        for nm, shape in (("b_Wq", [D, D]), ("b_Wk", [D, D]), ("b_Wv", [D, D]),
                          ("b_Wo", [D, D]), ("b_Wg1z", [1024, 1024]),
                          ("b_Wg1s", [16, 1024]), ("b_Wog1z", [1024, 512]),
                          ("b_Wog1s", [8, 512]), ("b_Wb", [P, 32]),
                          ("b_Wres", [P, 32]), ("b_Wg2", [P, 32]),
                          ("b_Wog2", [P, 4]), ("b_wqc", [P, 32]),
                          ("b_wkc", [P, 32]), ("b_wvc", [P, 32]),
                          ("b_firl", [P, 512]), ("b_firs", [P, 40]),
                          ("b_nw", [P, 2]), ("b_bg1", [P, NC]),
                          ("b_bog1", [P, 4]), ("b_bg2", [4, 1]),
                          ("b_bres", [4, 1]), ("b_crl", [4, 1]),
                          ("b_lt", [1, 1]), ("b_bog2", [1, 1])):
            wab[nm] = din(nm, shape)
        wscr = [nc.dram_tensor(f"w{i}_bf", [D, D if i < 5 else 512], BF16,
                               kind="Internal").ap() for i in range(6)]
        rbscr = {
            "rows": nc.dram_tensor("rows_scr", [64, TBB], BF16, kind="Internal").ap(),
            "ogr": nc.dram_tensor("ogr_scr", [32, TBB], BF16, kind="Internal").ap(),
            "w4": nc.dram_tensor("w4_scr", [64, 1024], BF16, kind="Internal").ap(),
            "w4og": nc.dram_tensor("w4og_scr", [32, 512], BF16, kind="Internal").ap(),
            "vin": nc.dram_tensor("vin_scr", [D, T], BF16, kind="Internal").ap(),
            "z": nc.dram_tensor("z_scr", [D, T], BF16, kind="Internal").ap(),
            "dl": nc.dram_tensor("dl_scr", [D, T], BF16, kind="Internal").ap(),
        }

    with es:
        C = {}
        S_C = r.sem("cload")
        for k, (shape, dt) in cspec.items():
            tns = es.enter_context(nc.sbuf_tensor("C_" + k, shape, dt))
            C[k] = tns[:]
            r.op("sync", lambda e, k=k, tns=tns: e.dma_start(out=tns[:], in_=cdr[k]),
                 inc=S_C, inc_n=16)
        for e in ENGS:
            r.wait(e, [(S_C, S_C.n)])
        r._bar_tile = C["bar"]
        r._bar_w = C["barw"]

        chain = [xin]
        bufs = [xA, xB]
        nb = 0
        for i, ph in enumerate(phases):
            dst = xout if i == len(phases) - 1 else bufs[nb % 2]
            nb += 1
            src = chain[-1]
            if ph == "attn_a":
                with ExitStack() as pes:
                    emit_attn_a(r, pes, nc, src, dst, ctx_scr, watt, C, f"A{i}")
            elif ph == "attn_b":
                with ExitStack() as pes:
                    emit_attn_b(r, pes, nc, src, dst, wab, wscr, rbscr, C, f"B{i}",
                                dbg=None)
            elif ph == "ffn":
                dbg = None
                if debug:
                    dbg = {
                        "z": nc.dram_tensor(f"dbg_z{i}", [P, NC * T], BF16, kind="ExternalOutput").ap(),
                        "scr": nc.dram_tensor(f"dbg_scr{i}", [1, T], F32, kind="ExternalOutput").ap(),
                        "rows": nc.dram_tensor(f"dbg_rows{i}", [1, 8 * TT], F32, kind="ExternalOutput").ap(),
                        "spp": nc.dram_tensor(f"dbg_spp{i}", [P, 32], F32, kind="ExternalOutput").ap(),
                        "bis": nc.dram_tensor(f"dbg_bis{i}", [P, 34], F32, kind="ExternalOutput").ap(),
                    }
                with ExitStack() as pes:
                    emit_ffn(r, pes, nc, src, dst, y_scr, wffn, C, f"F{i}", dbg=dbg)
            elif ph == "copy":
                S_CP = r.sem(f"cp{i}")
                tns = es.enter_context(nc.sbuf_tensor(f"cpbuf{i}", [P, NC * TT], F32))
                for t in range(NT):
                    wl = [(S_CP, S_CP.n)] if t > 0 else []
                    r.op("sync", lambda e, t=t, tns=tns, src=src: e.dma_start(
                        out=tns[:].rearrange("p (c q) -> p c q", c=NC),
                        in_=src.rearrange("(c p) t -> p c t", p=P)[:, :, t * TT:(t + 1) * TT]),
                        inc=S_CP, inc_n=16, wait=wl)
                    r.op("sync", lambda e, t=t, tns=tns, dst=dst: e.dma_start(
                        out=dst.rearrange("(c p) t -> p c t", p=P)[:, :, t * TT:(t + 1) * TT],
                        in_=tns[:].rearrange("p (c q) -> p c q", c=NC)),
                        inc=S_CP, inc_n=16, wait=[(S_CP, S_CP.n)])
                r.wait("sync", [(S_CP, S_CP.n)])
            else:
                raise ValueError(ph)
            chain.append(dst)
            r.barrier()

        r.replay()
    return nc


def _ones2h():
    a = np.zeros((P, 2), BF)
    a[0:64, 0] = 1
    a[64:128, 1] = 1
    return a


def _e4s():
    a = np.zeros((4, 512), BF)
    for h in range(4):
        a[h, h * 128:(h + 1) * 128] = 1
    return a


def _headsel():
    a = np.zeros((P, 32), BF)
    for c in range(8):
        a[:, c * 4 + c // 2] = 1
    return a


def _cntw():
    counts = np.zeros(T, np.float32)
    for w in range(15):
        counts[256 * w:256 * w + 512] += 1
    counts[3584:] += 1
    c = np.zeros((P, 64), np.float32)
    for b in range(32):
        c[:, b] = 1.0 / counts[128 * b:128 * (b + 1)]
        c[:, 32 + b] = 2.0 / counts[128 * b:128 * (b + 1)]
    return c


def make_consts():
    return {
        "c_ones128_b": np.ones((P, 1), BF),
        "c_ones128_f": np.ones((P, 1), np.float32),
        "c_ones1_b": np.ones((1, P), BF),
        "c_ones1_f": np.ones((1, P), np.float32),
        "c_ones1_f1": np.ones((1, 1), np.float32),
        "c_ident_f": np.eye(P, dtype=np.float32),
        "c_iota15": np.arange(1, 16, dtype=np.float32).reshape(1, 15),
        "c_bar": np.zeros((1, 4), np.float32),
        "c_barw": np.zeros((1, 1), BF),
        "c_ident_b": np.eye(P, dtype=BF),
        "c_ones2h": _ones2h(),
        "c_e4s": _e4s(),
        "c_headsel": _headsel(),
        "c_msl4": np.tile(-np.tril(np.ones((64, 64), np.float32), -1), (1, 4)),
        "c_msu4": np.tile(-np.triu(np.ones((64, 64), np.float32), 1), (1, 4)),
        "c_mui4": np.tile(np.triu(np.ones((64, 64)), 0).astype(BF), (1, 4)),
        "c_id4": np.tile(np.eye(64, dtype=BF), (1, 4)),
    }


def run_phases(x_full, inputs, phases, debug=False, raw=False):
    nc = build_program(phases, debug=debug)
    consts = make_consts()
    in_maps = []
    for b in range(8):
        m = dict(consts)
        m["xT"] = np.ascontiguousarray(x_full[b].T)
        m["w_routerL"] = np.ascontiguousarray(
            np.asarray(inputs["L0_ffn_w_router"]).reshape(NC, P).T)
        m["L0_ffn_w_block"] = np.asarray(inputs["L0_ffn_w_block"])
        if "attn_a" in phases:
            gi = lambda n: np.asarray(inputs["L0_attn_a_" + n])
            m["a_Wq"] = gi("Wq"); m["a_Wk"] = gi("Wk")
            m["a_Wv"] = gi("Wv"); m["a_Wo"] = gi("Wo")
            m["a_bq"] = np.ascontiguousarray(gi("bq").reshape(NC, P).T)
            m["a_bk"] = np.ascontiguousarray(gi("bk").reshape(NC, P).T)
            m["a_bv"] = np.ascontiguousarray(gi("bv").reshape(NC, P).T)
            m["a_bo"] = np.ascontiguousarray(gi("bo").reshape(NC, P).T)
            m["a_pbT"] = np.ascontiguousarray(gi("pos_bias").transpose(0, 2, 1))
            m["a_cntw"] = _cntw()
        if "attn_b" in phases:
            gb = lambda n: np.asarray(inputs["L0_attn_b_" + n])
            c32 = lambda a: np.ascontiguousarray(
                a.reshape(NC, P, -1).transpose(1, 0, 2).reshape(P, -1))
            m["b_Wq"] = gb("Wq"); m["b_Wk"] = gb("Wk"); m["b_Wv"] = gb("Wv")
            m["b_Wo"] = gb("Wo")
            m["b_Wg1z"] = np.ascontiguousarray(gb("Wg1")[0:1024])
            m["b_Wg1s"] = np.ascontiguousarray(gb("Wg1")[1024:1040])
            m["b_Wog1z"] = np.ascontiguousarray(gb("Wog1")[0:1024])
            m["b_Wog1s"] = np.ascontiguousarray(gb("Wog1")[1024:1032])
            m["b_Wb"] = c32(gb("Wb")); m["b_Wres"] = c32(gb("Wres"))
            m["b_Wg2"] = c32(gb("Wg2"))
            m["b_Wog2"] = np.ascontiguousarray(
                gb("Wog2").reshape(4, P).T)
            m["b_wqc"] = c32(gb("wq_conv")); m["b_wkc"] = c32(gb("wk_conv"))
            m["b_wvc"] = c32(gb("wv_conv"))
            m["b_firl"] = c32(gb("fir_long").reshape(1024, 64))
            m["b_firs"] = c32(gb("fir_short").reshape(1024, 5))
            m["b_nw"] = np.ascontiguousarray(gb("norm_w").reshape(2, P).T)
            m["b_bg1"] = np.ascontiguousarray(gb("bg1").reshape(NC, P).T)
            m["b_bog1"] = np.ascontiguousarray(gb("bog1").reshape(4, P).T)
            m["b_bg2"] = gb("bg2").reshape(4, 1)
            m["b_bres"] = gb("bres").reshape(4, 1)
            m["b_crl"] = gb("conv_res_logit").reshape(4, 1)
            m["b_lt"] = gb("logit_temp").reshape(1, 1)
            m["b_bog2"] = gb("bog2").reshape(1, 1)
        in_maps.append(m)
    res = run_bass_kernel_spmd(nc, in_maps, core_ids=list(range(8)))
    out = np.stack([res.results[b]["outT"].T for b in range(8)], axis=0)
    if raw:
        return np.ascontiguousarray(out), res
    return np.ascontiguousarray(out)


def kernel(**inputs):
    x0 = np.asarray(inputs["L0_attn_a_hidden_states"])
    return run_phases(x0, inputs, ["attn_a", "ffn", "attn_b", "ffn"])


if __name__ == "__main__":
    pass
